# version 09
# baseline (speedup 1.0000x reference)
import contextlib

import ml_dtypes
import numpy as np

import concourse.bass as bass
import concourse.mybir as mybir
from concourse.bass_utils import run_bass_kernel_spmd

F32 = mybir.dt.float32
BF16 = mybir.dt.bfloat16
AF = mybir.ActivationFunctionType
OP = mybir.AluOpType
AX = mybir.AxisListType.X

DM, DI, DN, DR, DC = 2048, 4096, 16, 128, 4
NE, NI = 8, 1024
LSEQ, NB = 8192, 2
LC = 512
NDB = DI // 128
NKD = DM // 128
KCAT = NE * NI + NI
NKC = KCAT // 128
EPS = 1e-5
NPRE_DEF, NOWN_DEF = 12, 4


class Sem:
    def __init__(self, h):
        self.h = h
        self.n = 0


def pmax(*pts):
    pts = [p for p in pts if p is not None]
    if not pts:
        return None
    return tuple(max(p[i] for p in pts) for i in range(5))


SERIAL = True


class Builder:

    ENGS = ("sync", "gpsimd", "vector", "scalar", "tensor")
    EID = {e: i for i, e in enumerate(ENGS)}

    def __init__(self):
        self.q = {e: [] for e in self.ENGS}
        self.last = {}
        self.esems = None
        self.ecnt = [0, 0, 0, 0, 0]
        self.hw = [[0] * 5 for _ in range(5)]
        self.gtot = 0
        self.vfence = None

    def pt(self):
        return tuple(self.ecnt)

    def _emit_waits(self, eid, tgt):
        ws = []
        for y in range(5):
            if y == eid and eid != 2:
                continue
            v = tgt[y]
            if v > self.hw[eid][y]:
                ws.append((self.esems[y], v))
                self.hw[eid][y] = v
        return ws

    def wait(self, eng, sem, val):
        if SERIAL:
            return
        if val is None or val <= 0:
            return
        self.q[eng].append(lambda e, s=sem.h, v=val: e.wait_ge(s, v))

    def op(self, eng, fn, sem=None, upto=None):
        v = None
        if sem is not None:
            sem.n += 1
            v = sem.n

        if SERIAL:
            eid = self.EID[eng]
            tgt = upto if upto is not None else self.pt()
            ws = self._emit_waits(eid, tgt)
            fence = (eng == "vector")
            me = self
            mysem = self.esems[eid]
            self.ecnt[eid] += 1
            self.gtot += 1

            def g(e, fn=fn, ws=ws, fence=fence, me=me, mysem=mysem):
                for s, val_ in ws:
                    e.wait_ge(s.h, val_)
                ins = fn(e)
                assert ins is not None, "op fn must return its last instruction"
                if fence:
                    ins = e.memset(me.vfence, 0.0)
                ins.then_inc(mysem.h, 1)
            self.q[eng].append(g)
            return v

        def g(e, fn=fn, sem=sem):
            ins = fn(e)
            if sem is not None:
                ins.then_inc(sem.h, 1)
        self.q[eng].append(g)
        return v

    def dma(self, eng, out, in_, sem, upto=None):
        sem.n += 16
        v = sem.n

        if SERIAL:
            eid = self.EID[eng]
            tgt = upto if upto is not None else self.pt()
            own_prev = self.ecnt[eid]
            ws = self._emit_waits(eid, tgt)
            mysem = self.esems[eid]
            self.ecnt[eid] += 16
            self.gtot += 16

            def g(e, out=out, in_=in_, ws=ws, mysem=mysem, own_prev=own_prev):
                for s, val_ in ws:
                    e.wait_ge(s.h, val_)
                if own_prev > 0:
                    e.wait_ge(mysem.h, own_prev)
                e.dma_start(out=out, in_=in_).then_inc(mysem.h, 16)
            self.q[eng].append(g)
            return v

        def g(e, out=out, in_=in_, sem=sem):
            e.dma_start(out=out, in_=in_).then_inc(sem.h, 16)
        self.q[eng].append(g)
        return v


def build(npre=NPRE_DEF, nown=NOWN_DEF, dbg=False):
    nt = npre + nown
    ntok = nt * LC
    nc = bass.Bass("TRN2", target_bir_lowering=False, debug=False,
                   detect_race_conditions=False)

    xin = nc.dram_tensor("xin", [ntok, DM], F32, kind="ExternalInput").ap()
    wxT = nc.dram_tensor("wxT", [NDB, 128, 16, 128], BF16, kind="ExternalInput").ap()
    wzT = nc.dram_tensor("wzT", [NDB, 128, 16, 128], BF16, kind="ExternalInput").ap()
    xpT = nc.dram_tensor("xpT", [128, NDB, 160], BF16, kind="ExternalInput").ap()
    dtTw = nc.dram_tensor("dtTw", [128, NDB, 128], BF16, kind="ExternalInput").ap()
    woT = nc.dram_tensor("woT", [NKD, 128, 32, 128], BF16, kind="ExternalInput").ap()
    w1T = nc.dram_tensor("w1T", [NE, 8, 128, 16, 128], BF16, kind="ExternalInput").ap()
    w3T = nc.dram_tensor("w3T", [NE, 8, 128, 16, 128], BF16, kind="ExternalInput").ap()
    w2T = nc.dram_tensor("w2T", [NKD, 128, NKC, 128], BF16, kind="ExternalInput").ap()
    ws1T = nc.dram_tensor("ws1T", [8, 128, 16, 128], BF16, kind="ExternalInput").ap()
    ws3T = nc.dram_tensor("ws3T", [8, 128, 16, 128], BF16, kind="ExternalInput").ap()
    gwTi = nc.dram_tensor("gwTi", [128, 16, NE], BF16, kind="ExternalInput").ap()
    alog = nc.dram_tensor("alog", [128, NDB * DN], F32, kind="ExternalInput").ap()
    dtbi = nc.dram_tensor("dtbi", [128, NDB], F32, kind="ExternalInput").ap()
    dvec = nc.dram_tensor("dvec", [128, NDB], F32, kind="ExternalInput").ap()
    cwin = nc.dram_tensor("cwin", [128, NDB * DC], F32, kind="ExternalInput").ap()
    gbin = nc.dram_tensor("gbin", [1, NE], F32, kind="ExternalInput").ap()
    yout = nc.dram_tensor("yout", [nown * LC, DM], F32, kind="ExternalOutput").ap()

    spk = "ExternalOutput" if dbg else "Internal"
    if dbg:
        dbg_xc = nc.dram_tensor("dbg_xc", [NDB, 128, LC], BF16, kind="ExternalOutput").ap()
        dbg_bc = nc.dram_tensor("dbg_bc", [32, LC], F32, kind="ExternalOutput").ap()
        dbg_dl = nc.dram_tensor("dbg_dl", [2, 128, LC], F32, kind="ExternalOutput").ap()
        dbg_da = nc.dram_tensor("dbg_da", [DN, 128, LC], F32, kind="ExternalOutput").ap()
        dbg_h = nc.dram_tensor("dbg_h", [DN, 128, LC], BF16, kind="ExternalOutput").ap()
        dbg_A = nc.dram_tensor("dbg_A", [128, NDB * DN], F32, kind="ExternalOutput").ap()
        dbg_cr = nc.dram_tensor("dbg_cr", [16, 128, NDB * DN], F32, kind="ExternalOutput").ap()
        dbg_dw = nc.dram_tensor("dbg_dw", [8, LC], BF16, kind="ExternalOutput").ap()
        dbg_dwb = nc.dram_tensor("dbg_dwb", [128, 8, LC], BF16, kind="ExternalOutput").ap()
        dbg_gt = nc.dram_tensor("dbg_gt", [4, 128, 120], F32, kind="ExternalOutput").ap()
        dbg_snapg = nc.dram_tensor("dbg_snapg", [128, 16, 128], BF16, kind="ExternalOutput").ap()
        dbg_snapP = nc.dram_tensor("dbg_snapP", [128, 16, 128], BF16, kind="ExternalOutput").ap()
        dbg_sc = nc.dram_tensor("dbg_sc", [4, 128, LC], F32, kind="ExternalOutput").ap()
    sp_xn1 = nc.dram_tensor("sp_xn1", [nown, 16, 128, LC], BF16, kind=spk).ap()
    sp_t1y = nc.dram_tensor("sp_t1y", [nown, NDB, 128, LC], BF16, kind=spk).ap()
    sp_x2 = nc.dram_tensor("sp_x2", [nown, 4, 128, DM], F32, kind=spk).ap()
    sp_hc = nc.dram_tensor("sp_hc", [nown, NKC, 128, LC], BF16, kind=spk).ap()

    b = Builder()
    ctx = contextlib.ExitStack()
    with ctx:
        b.esems = [Sem(ctx.enter_context(nc.semaphore(f"s_e{i}")))
                   for i in range(5)]
        s_ld = Sem(ctx.enter_context(nc.semaphore("s_ld")))
        s_st = Sem(ctx.enter_context(nc.semaphore("s_st")))
        s_v = Sem(ctx.enter_context(nc.semaphore("s_v")))
        s_s = Sem(ctx.enter_context(nc.semaphore("s_s")))
        s_t = Sem(ctx.enter_context(nc.semaphore("s_t")))
        s_g = Sem(ctx.enter_context(nc.semaphore("s_g")))
        s_w0 = Sem(ctx.enter_context(nc.semaphore("s_w0")))
        s_w1 = Sem(ctx.enter_context(nc.semaphore("s_w1")))
        s_w = [s_w0, s_w1]
        s_x = [Sem(ctx.enter_context(nc.semaphore("s_x0"))),
               Sem(ctx.enter_context(nc.semaphore("s_x1")))]
        s_t1s = [Sem(ctx.enter_context(nc.semaphore("s_t1s0"))),
                 Sem(ctx.enter_context(nc.semaphore("s_t1s1")))]
        s_hd = [Sem(ctx.enter_context(nc.semaphore("s_hd0"))),
                Sem(ctx.enter_context(nc.semaphore("s_hd1")))]
        s_x2s = Sem(ctx.enter_context(nc.semaphore("s_x2s")))
        s_xn1s = Sem(ctx.enter_context(nc.semaphore("s_xn1s")))
        s_xn1l = Sem(ctx.enter_context(nc.semaphore("s_xn1l")))
        s_ti = [Sem(ctx.enter_context(nc.semaphore("s_ti0"))),
                Sem(ctx.enter_context(nc.semaphore("s_ti1")))]
        s_xo = [Sem(ctx.enter_context(nc.semaphore("s_xo0"))),
                Sem(ctx.enter_context(nc.semaphore("s_xo1")))]
        s_hcl = Sem(ctx.enter_context(nc.semaphore("s_hcl")))
        s_ob = [Sem(ctx.enter_context(nc.semaphore("s_ob0"))),
                Sem(ctx.enter_context(nc.semaphore("s_ob1")))]

        vfence_t = ctx.enter_context(nc.sbuf_tensor("vfence", [128, 1], F32))
        b.vfence = vfence_t[:]
        ident = ctx.enter_context(nc.sbuf_tensor("ident", [128, 128], BF16))
        ident_f32 = ctx.enter_context(nc.sbuf_tensor("ident_f32", [128, 128], F32))
        ones1 = ctx.enter_context(nc.sbuf_tensor("ones1", [1, 128], BF16))
        ones8 = ctx.enter_context(nc.sbuf_tensor("ones8", [128, 8], F32))
        A_sb = ctx.enter_context(nc.sbuf_tensor("A_sb", [128, NDB * DN], F32))
        dtb_sb = ctx.enter_context(nc.sbuf_tensor("dtb_sb", [128, NDB], F32))
        D_sb = ctx.enter_context(nc.sbuf_tensor("D_sb", [128, NDB], F32))
        cw_sb = ctx.enter_context(nc.sbuf_tensor("cw_sb", [128, NDB * DC], F32))
        carry = ctx.enter_context(nc.sbuf_tensor("carry", [128, NDB * DN], F32))
        gb_bc = ctx.enter_context(nc.sbuf_tensor("gb_bc", [128, 8], F32))
        gbrow = ctx.enter_context(nc.sbuf_tensor("gbrow", [1, 8], BF16))
        xpT_sb = ctx.enter_context(nc.sbuf_tensor("xpT_sb", [128, NDB, 160], BF16))
        dtT_sb = ctx.enter_context(nc.sbuf_tensor("dtT_sb", [128, NDB, 128], BF16))
        gwT_sb = ctx.enter_context(nc.sbuf_tensor("gwT_sb", [128, 16, 8], BF16))
        halo = ctx.enter_context(nc.sbuf_tensor("halo", [128, NDB, 4], BF16))
        lnb = ctx.enter_context(nc.sbuf_tensor("lnb", [128, 16], F32))
        epsc = ctx.enter_context(nc.sbuf_tensor("epsc", [128, 1], F32))
        onec = ctx.enter_context(nc.sbuf_tensor("onec", [128, 1], F32))
        m30 = ctx.enter_context(nc.sbuf_tensor("m30", [128, 1], F32))
        xn_bf = ctx.enter_context(nc.sbuf_tensor("xn_bf", [128, 2, DM], BF16))
        xnT = ctx.enter_context(nc.sbuf_tensor("xnT", [128, 16, LC], BF16))
        wbufA = ctx.enter_context(nc.sbuf_tensor("wbufA", [128, 2, 16, 128], BF16))
        xc_sb = ctx.enter_context(nc.sbuf_tensor("xc_sb", [128, NDB, LC], BF16))
        Bb_sb = ctx.enter_context(nc.sbuf_tensor("Bb_sb", [128, 16, LC], BF16))
        Cb_sb = ctx.enter_context(nc.sbuf_tensor("Cb_sb", [128, 16, LC], BF16))
        sel32 = ctx.enter_context(nc.sbuf_tensor("sel32", [32, 32, 128], BF16))
        sel8 = ctx.enter_context(nc.sbuf_tensor("sel8", [8, 8, 128], BF16))

        PSA = ctx.enter_context(nc.psum_tensor("PSA", [128, 512], F32))
        PSB = ctx.enter_context(nc.psum_tensor("PSB", [128, 512], F32))
        PSD = ctx.enter_context(nc.psum_tensor("PSD", [128, 512], F32))
        PSE = ctx.enter_context(nc.psum_tensor("PSE", [128, 512], F32))
        PSJ = ctx.enter_context(nc.psum_tensor("PSJ", [128, 512], F32))
        PST1 = ctx.enter_context(nc.psum_tensor("PST1", [128, 512], F32))
        PST2 = ctx.enter_context(nc.psum_tensor("PST2", [128, 512], F32))

        L = b.last
        P = b.last

        def flush_pe(e):
            e.matmul(PSJ[:, 0:LC], ident[:], Bb_sb[:, 0, :], start=True, stop=True)
            return e.matmul(PSJ[:, 0:LC], ident[:], Bb_sb[:, 0, :], start=True,
                            stop=True)

        b.op("gpsimd", lambda e: e.memset(ident[:], 0.0), s_g)
        b.op("gpsimd", lambda e: e.memset(ident_f32[:], 0.0), s_g)
        b.op("gpsimd", lambda e: e.memset(ones1[:], 1.0), s_g)
        b.op("gpsimd", lambda e: e.memset(ones8[:], 1.0), s_g)
        b.op("gpsimd", lambda e: e.memset(halo[:], 0.0), s_g)
        b.op("gpsimd", lambda e: e.memset(carry[:], 0.0), s_g)
        b.op("gpsimd", lambda e: e.memset(epsc[:], EPS), s_g)
        b.op("gpsimd", lambda e: e.memset(onec[:], 1.0), s_g)
        b.op("gpsimd", lambda e: e.memset(m30[:], -1e30), s_g)
        b.op("gpsimd", lambda e: e.memset(lnb[:], 0.0), s_g)
        b.op("gpsimd", lambda e: e.memset(gt_sb[:], 123.0), s_g)
        b.op("gpsimd", lambda e: e.memset(sel32[:], 0.0), s_g)
        b.op("gpsimd", lambda e: e.memset(sel8[:], 0.0), s_g)
        b.wait("gpsimd", s_g, s_g.n)
        b.op("gpsimd", lambda e: e.affine_select(
            out=ident[:], in_=ident[:], compare_op=OP.not_equal, fill=1.0,
            base=0, pattern=[[-1, 128]], channel_multiplier=1), s_g)
        b.op("gpsimd", lambda e: e.affine_select(
            out=ident_f32[:], in_=ident_f32[:], compare_op=OP.not_equal, fill=1.0,
            base=0, pattern=[[-1, 128]], channel_multiplier=1), s_g)
        for n in range(32):
            b.op("gpsimd", lambda e, n=n: e.affine_select(
                out=sel32[:, n, :], in_=sel32[:, n, :],
                compare_op=OP.not_equal, fill=1.0, base=-n,
                pattern=[[0, 128]], channel_multiplier=1), s_g)
        for n in range(8):
            b.op("gpsimd", lambda e, n=n: e.affine_select(
                out=sel8[:, n, :], in_=sel8[:, n, :],
                compare_op=OP.not_equal, fill=1.0, base=-n,
                pattern=[[0, 128]], channel_multiplier=1), s_g)
        consts_done = s_g.n

        b.dma("sync", A_sb[:], alog, s_ld)
        b.dma("sync", dtb_sb[:], dtbi, s_ld)
        b.dma("sync", D_sb[:], dvec, s_ld)
        b.dma("sync", cw_sb[:], cwin, s_ld)
        b.dma("gpsimd", gbrow[0:1, :], gbin, s_w0)
        b.wait("gpsimd", s_w0, s_w0.n)
        b.wait("scalar", s_ld, s_ld.n)
        b.op("scalar", lambda e: e.activation(spt[:], A_sb[:], AF.Exp, bias=0.0, scale=1.0), s_s)
        b.wait("vector", s_s, s_s.n)
        b.op("vector", lambda e: e.tensor_scalar_mul(A_sb[:], spt[:], -1.0), s_v)
        b.wait("tensor", s_ld, s_ld.n)
        b.wait("tensor", s_w0, s_w0.n)
        tv = b.op("tensor", lambda e: e.matmul(PSD[:, 0:8], ones1[:], gbrow[0:1, :],
                                               start=True, stop=True), s_t)
        b.wait("scalar", s_t, tv)
        b.op("scalar", lambda e: e.copy(gb_bc[:], PSD[:, 0:8]), s_s)

        x_rg = ctx.enter_context(nc.sbuf_tensor("x_rg", [128, 2, DM], F32))
        spt = ctx.enter_context(nc.sbuf_tensor("spt", [128, LC], F32))
        delta_r = ctx.enter_context(nc.sbuf_tensor("delta_r", [128, 2, LC], F32))
        dA_r = ctx.enter_context(nc.sbuf_tensor("dA_r", [128, 3, LC], F32))
        dBu_sb = ctx.enter_context(nc.sbuf_tensor("dBu_sb", [128, LC], BF16))
        h32_r = ctx.enter_context(nc.sbuf_tensor("h32_r", [128, 2, LC], F32))
        hbf_r = ctx.enter_context(nc.sbuf_tensor("hbf_r", [128, 4, LC], BF16))
        tmp_r = ctx.enter_context(nc.sbuf_tensor("tmp_r", [128, 2, LC], BF16))
        cvo_r = ctx.enter_context(nc.sbuf_tensor("cvo_r", [128, 2, LC], F32))
        xzx_r = ctx.enter_context(nc.sbuf_tensor("xzx_r", [128, 2, LC + 4], BF16))
        dtl_sb = ctx.enter_context(nc.sbuf_tensor("dtl_sb", [128, LC], BF16))
        bc_sb = ctx.enter_context(nc.sbuf_tensor("bc_sb", [32, LC], F32))
        bcb_sb = ctx.enter_context(nc.sbuf_tensor("bcb_sb", [32, LC], BF16))
        u_r = ctx.enter_context(nc.sbuf_tensor("u_r", [128, 2, LC], BF16))
        t1y_r = ctx.enter_context(nc.sbuf_tensor("t1y_r", [128, 2, LC], BF16))
        cva = ctx.enter_context(nc.sbuf_tensor("cva", [128, LC], F32))
        wbufW = ctx.enter_context(nc.sbuf_tensor("wbufW", [128, 80, 128], BF16))
        mo_r = Cb_sb[:, 0:2, :]
        sz_r = Cb_sb[:, 2:4, :]
        t1yi = Cb_sb[:, 4:6, :]
        dwT_sb_full = Cb_sb[:, 6:7, :]
        dwb_sb = Cb_sb[:, 8:16, :]
        gt_sb = ctx.enter_context(nc.sbuf_tensor("gt_sb", [128, 128], F32))
        if True:

            if dbg:
                b.wait("sync", s_v, s_v.n)
                b.dma("sync", dbg_A, A_sb[:], s_ld)
            b.dma("gpsimd", xpT_sb[:], xpT, s_w0)
            b.wait("gpsimd", s_w0, s_w0.n)
            b.dma("gpsimd", dtT_sb[:], dtTw, s_w0)
            b.wait("gpsimd", s_w0, s_w0.n)
            b.dma("gpsimd", gwT_sb[:], gwTi, s_w0)
            b.wait("vector", s_w0, s_w0.n)
            b.wait("vector", s_g, consts_done)
            vready = b.op("vector", lambda e: e.tensor_copy(lnb[:, 9:10], lnb[:, 8:9]), s_v)
            for eng in ("scalar", "tensor", "sync", "gpsimd"):
                b.wait(eng, s_v, vready)

            for tile in range(nt):
                own = tile >= npre
                ot = tile - npre
                if dbg:
                    b.wait("sync", s_v, s_v.n)
                    b.dma("sync", dbg_cr[tile], carry[:], s_ld)
                    b.wait("vector", s_ld, s_ld.n)
                b.wait("scalar", s_xn1s, s_xn1s.n)
                k = L.get("scanv")
                if k is not None:
                    b.wait("scalar", s_v, k)
                for tb in range(4):
                    xs = tile * LC + tb * 128
                    sl = tb % 2
                    k = L.get(("xrg", sl))
                    if k is not None:
                        b.wait("sync", s_v, k)
                    b.dma("sync", x_rg[:, sl, :], xin[xs:xs + 128, :], s_x[sl])
                    ldv = s_x[sl].n
                    b.wait("scalar", s_x[sl], ldv)
                    k = L.get("sq_v")
                    if k is not None:
                        b.wait("scalar", s_v, k)
                    k = L.get(("xn", sl))
                    if k is not None:
                        b.wait("scalar", s_t, k)
                    sqv = b.op("scalar", lambda e, sl=sl: e.activation(
                        xn_bf[:, sl, :], x_rg[:, sl, :], AF.Square, bias=0.0,
                        scale=1.0, accum_out=lnb[:, 1:2]), s_s)
                    b.wait("vector", s_x[sl], ldv)
                    b.wait("vector", s_s, sqv)

                    b.op("vector", lambda e, sl=sl: e.reduce_sum(
                        lnb[:, 0:1], x_rg[:, sl, :], axis=AX), s_v)
                    b.op("vector", lambda e: e.tensor_scalar_mul(
                        lnb[:, 2:3], lnb[:, 0:1], 1.0 / DM), s_v)
                    b.op("vector", lambda e: e.tensor_scalar_mul(
                        lnb[:, 3:4], lnb[:, 1:2], 1.0 / DM), s_v)
                    b.op("vector", lambda e: e.tensor_tensor(
                        lnb[:, 4:5], lnb[:, 2:3], lnb[:, 2:3], OP.mult), s_v)
                    lv = b.op("vector", lambda e: e.tensor_tensor(
                        lnb[:, 5:6], lnb[:, 3:4], lnb[:, 4:5], OP.subtract), s_v)
                    L["sq_v"] = lv

                    def rstd(e):
                        e.activation(lnb[:, 6:7], lnb[:, 5:6], AF.Ln, bias=epsc[:, 0:1], scale=1.0)
                        return e.activation(lnb[:, 7:8], lnb[:, 6:7], AF.Exp, bias=0.0,
                                            scale=-0.5)
                    b.wait("scalar", s_v, lv)
                    rsv = b.op("scalar", rstd, s_s)
                    b.wait("vector", s_s, rsv)
                    k = L.get(("xn", sl))
                    if k is not None:
                        b.wait("vector", s_t, k)
                    xnv = b.op("vector", lambda e, sl=sl: e.tensor_scalar(
                        xn_bf[:, sl, :], x_rg[:, sl, :], lnb[:, 2:3], lnb[:, 7:8],
                        OP.subtract, OP.mult), s_v)
                    L[("xrg", sl)] = xnv
                    xnv_pt = b.pt()
                    for g in range(4):
                        pst = PST1 if g % 2 == 0 else PST2

                        def tr(e, g=g, sl=sl, pst=pst):
                            ins = None
                            for j in range(4):
                                dc = g * 4 + j
                                ins = e.matmul(pst[:, j * 128:(j + 1) * 128],
                                               xn_bf[:, sl, dc * 128:(dc + 1) * 128],
                                               ident[:], start=True, stop=True)
                            return ins
                        tv = b.op("tensor", tr, s_t, upto=pmax(
                            xnv_pt, P.get(("LNpst", g % 2))))

                        def ev(e, g=g, tb=tb, pst=pst):
                            return e.copy(
                                xnT[:, g * 4:(g + 1) * 4, tb * 128:tb * 128 + 128],
                                pst[:, 0:512].rearrange("p (a c) -> p a c", a=4))
                        evv = b.op("scalar", ev, s_s)
                        P[("LNpst", g % 2)] = b.pt()
                        L[("pst", g % 2)] = evv
                        L["xnT_wr"] = evv
                    L[("xn", sl)] = s_t.n
                s_xnT = s_s.n

                if own:
                    b.wait("sync", s_s, s_xnT)
                    for g in range(16):
                        L["sp1"] = b.dma("sync", sp_xn1[ot, g],
                                         xnT[:, g, :], s_xn1s)

                ln_pt = b.pt()
                P = b.last
                for mblk in range(NDB):
                    ws = mblk % 2
                    b.dma("gpsimd", wbufW[:, ws * 16:(ws + 1) * 16, :],
                          wxT[mblk], s_w[ws], upto=P.get(("Awrd", ws)))
                    P[("Awld", ws)] = b.pt()
                    ps = PSA if mblk % 2 == 0 else PSB

                    def mmx(e, ws=ws, ps=ps):
                        ins = None
                        for kc in range(16):
                            ins = e.matmul(ps[:, 0:LC], wbufW[:, ws * 16 + kc, :], xnT[:, kc, :],
                                           start=(kc == 0), stop=(kc == 15))
                        return ins
                    tv = b.op("tensor", mmx, s_t, upto=pmax(
                        P[("Awld", ws)], ln_pt, P.get(("Aev", mblk % 2))))
                    P[("Awrd", ws)] = b.pt()
                    xsl = mblk % 2

                    def evx(e, sl=xsl, ps=ps, db=mblk):
                        e.copy(xzx_r[:, sl, 4:4 + LC], ps[:, 0:LC])
                        return e.copy(xzx_r[:, sl, 1:4], halo[:, db, 1:4])
                    evv = b.op("scalar", evx, s_s, upto=pmax(
                        P[("Awrd", ws)], P.get(("Acv", xsl))))
                    P[("Aev", mblk % 2)] = b.pt()

                    def svh(e, sl=xsl, db=mblk):
                        return e.copy(halo[:, db, 1:4], xzx_r[:, sl, 1 + LC:4 + LC])
                    sv = b.op("scalar", svh, s_s, upto=P[("Aev", mblk % 2)])
                    svh_pt = b.pt()
                    csl = mblk % 2

                    b.op("vector", lambda e, sl=xsl, db=mblk: e.tensor_scalar(
                        cva[:], xzx_r[:, sl, 1:1 + LC],
                        cw_sb[:, db * 4:db * 4 + 1], None, OP.mult), s_v,
                        upto=pmax(svh_pt, P.get(("Asl", csl))))
                    b.op("vector", lambda e, sl=xsl, db=mblk, cs=csl:
                         e.scalar_tensor_tensor(
                             cvo_r[:, cs, :], xzx_r[:, sl, 2:2 + LC],
                             cw_sb[:, db * 4 + 1:db * 4 + 2],
                             cva[:], OP.mult, OP.add), s_v)
                    b.op("vector", lambda e, sl=xsl, db=mblk, cs=csl:
                         e.scalar_tensor_tensor(
                             cva[:], xzx_r[:, sl, 3:3 + LC],
                             cw_sb[:, db * 4 + 2:db * 4 + 3],
                             cvo_r[:, cs, :], OP.mult, OP.add), s_v)
                    cvv = b.op("vector", lambda e, sl=xsl, db=mblk, cs=csl:
                               e.scalar_tensor_tensor(
                                   cvo_r[:, cs, :], xzx_r[:, sl, 4:4 + LC],
                                   cw_sb[:, db * 4 + 3:db * 4 + 4], cva[:],
                                   OP.mult, OP.add), s_v)
                    P[("Acv", xsl)] = b.pt()
                    scv = b.op("scalar", lambda e, db=mblk, cs=csl: e.activation(
                        xc_sb[:, db, :], cvo_r[:, cs, :], AF.Silu, bias=0.0, scale=1.0),
                        s_s, upto=P[("Acv", xsl)])
                    P[("Asl", csl)] = b.pt()
                s_xc = s_s.n
                if dbg and tile == nt - 1:
                    b.wait("sync", s_s, s_xc)
                    for db in range(NDB):
                        b.dma("sync", dbg_xc[db], xc_sb[:, db, :], s_ld)

                b.wait("tensor", s_s, s_xc)

                def xd1(e):
                    ins = None
                    for kc in range(NDB):
                        ins = e.matmul(PSA[:, 0:LC], xpT_sb[:, kc, 0:128],
                                       xc_sb[:, kc, :], start=(kc == 0),
                                       stop=(kc == NDB - 1))
                    return ins
                tv1 = b.op("tensor", xd1, s_t)
                b.wait("scalar", s_t, tv1)
                dtlv = b.op("scalar", lambda e: e.copy(dtl_sb[:], PSA[:, 0:LC]), s_s)

                def xd2(e):
                    ins = None
                    for kc in range(NDB):
                        ins = e.matmul(PSB[0:32, 0:LC], xpT_sb[:, kc, 128:160],
                                       xc_sb[:, kc, :], start=(kc == 0),
                                       stop=(kc == NDB - 1))
                    return ins
                tv2 = b.op("tensor", xd2, s_t)
                b.wait("scalar", s_t, tv2)
                b.op("scalar", lambda e: e.copy(bc_sb[:], PSB[0:32, 0:LC]), s_s)
                bcv2 = b.op("vector", lambda e: e.tensor_copy(bcb_sb[:], bc_sb[:]), s_v)
                if dbg and tile == nt - 1:
                    b.wait("sync", s_v, bcv2)
                    b.dma("sync", dbg_bc, bc_sb[:], s_ld)
                bcb_pt = b.pt()
                sscall = pmax(P.get(("Ssc", 0)), P.get(("Ssc", 1)),
                              P.get(("Ssc", 2)))
                nrows = 32 if own else 16
                for n in range(nrows):
                    psb_ = PSD if n % 2 == 0 else PSE

                    def bbm(e, n=n, psb_=psb_):
                        return e.matmul(psb_[:, 0:LC], sel32[:, n, :], bcb_sb[:],
                                        start=True, stop=True)
                    tv = b.op("tensor", bbm, s_t, upto=pmax(
                        bcb_pt, P.get(("Bev", n % 2)), P.get("St1c")))
                    bbm_pt = b.pt()
                    dst = Bb_sb[:, n, :] if n < 16 else Cb_sb[:, n - 16, :]
                    bev = b.op("scalar", lambda e, d=dst, psb_=psb_: e.copy(
                        d, psb_[:, 0:LC]), s_s, upto=pmax(bbm_pt, sscall))
                    P[("Bev", n % 2)] = b.pt()
                s_bc = s_s.n
                b.wait("vector", s_s, s_bc)
                bc_pt = b.pt()

                for db in range(NDB):
                    dsl = db % 2
                    ps = PSA if db % 2 == 0 else PSB
                    k = L.get(("dtev", dsl))
                    if k is not None:
                        b.wait("tensor", s_s, k)
                    if db == 0:
                        b.wait("tensor", s_s, dtlv)
                    if db == 1:
                        b.wait("tensor", s_v, bcv2)

                    def dtm(e, db=db, ps=ps):
                        return e.matmul(ps[:, 0:LC], dtT_sb[:, db, :], dtl_sb[:],
                                        start=True, stop=True)
                    tv = b.op("tensor", dtm, s_t, upto=pmax(
                        bc_pt, P.get(("Sdt", dsl))))
                    dtm_pt = b.pt()

                    def softp(e, db=db, sl=dsl, ps=ps):
                        e.activation(spt[:], ps[:, 0:LC], AF.Exp,
                                     bias=dtb_sb[:, db:db + 1], scale=1.0)
                        return e.activation(delta_r[:, sl, :], spt[:], AF.Ln,
                                            bias=onec[:, 0:1], scale=1.0)
                    spv = b.op("scalar", softp, s_s, upto=pmax(
                        dtm_pt, P.get(("Suv", dsl))))
                    P[("Sdt", dsl)] = b.pt()
                    spv_pt = b.pt()
                    L[("dtev", dsl)] = spv
                    if dbg and tile == nt - 1 and db < 2:
                        b.wait("sync", s_s, spv)
                        b.dma("sync", dbg_dl[db], delta_r[:, dsl, :], s_ld)
                    usl = db % 2
                    uv = b.op("vector", lambda e, sl=dsl, us=usl, db=db: e.tensor_tensor(
                        u_r[:, us, :], delta_r[:, sl, :], xc_sb[:, db, :], OP.mult), s_v,
                        upto=pmax(spv_pt, bc_pt))
                    P[("Suv", dsl)] = b.pt()
                    L[("delta", dsl)] = s_v.n

                    lastdelta_use = None
                    for n in range(DN):
                        das = n % 3
                        col = db * DN + n
                        dav = b.op("scalar", lambda e, sl=dsl, da=das, col=col:
                                   e.activation(dA_r[:, da, :], delta_r[:, sl, :],
                                                AF.Exp, bias=0.0,
                                                scale=A_sb[:, col:col + 1]), s_s,
                                   upto=pmax(P.get(("Ssc", das)), bc_pt))
                        dav_pt = b.pt()
                        if dbg and tile == nt - 1 and db == 0:
                            b.wait("sync", s_s, dav)
                            dmpv = b.dma("sync", dbg_da[n], dA_r[:, das, :], s_ld)
                            b.wait("scalar", s_ld, dmpv)
                        if own:
                            hs = n % 4
                            ts = n % 2

                            def sct(e, us=usl, n=n, da=das, hs=hs, col=col,
                                    ts=ts):
                                e.tensor_tensor(dBu_sb[:], u_r[:, us, :],
                                                Bb_sb[:, n, :], OP.mult)
                                e.tensor_tensor_scan(
                                    hbf_r[:, hs, :], dA_r[:, da, :], dBu_sb[:],
                                    carry[:, col:col + 1], OP.mult, OP.add)
                                return e.tensor_tensor(
                                    tmp_r[:, ts, :], hbf_r[:, hs, :],
                                    Cb_sb[:, n, :], OP.mult)
                            scv = b.op("vector", sct, s_v, upto=pmax(
                                dav_pt, bc_pt, P.get(("Sy", ts))))
                            P[("Ssc", das)] = b.pt()
                            sct_pt = b.pt()
                            L[("dA", das)] = scv
                            if dbg and tile == nt - 1 and db == 0:
                                b.wait("sync", s_v, scv)
                                b.dma("sync", dbg_h[n], hbf_r[:, hs, :], s_ld)
                            if n % 4 == 3:
                                b.op("vector", lambda e, db=db, n0=n - 3: e.tensor_copy(
                                    carry[:, db * DN + n0:db * DN + n0 + 4],
                                    hbf_r[:, :, LC - 1:LC]),
                                    s_v, upto=sct_pt)
                            yv = b.op("tensor", lambda e, n=n, ts=ts: e.matmul(
                                PSE[:, 0:LC], ident[:], tmp_r[:, ts, :],
                                start=(n == 0), stop=(n == DN - 1)), s_t,
                                upto=pmax(sct_pt,
                                          pmax(P.get("St1c"),
                                               P.get(("Bev", 1)))
                                          if n == 0 else None))
                            P[("Sy", ts)] = b.pt()
                            L[("tmp", ts)] = yv
                        else:
                            hs = n % 2

                            def sct(e, us=usl, n=n, da=das, hs=hs, col=col):
                                e.tensor_tensor(dBu_sb[:], u_r[:, us, :],
                                                Bb_sb[:, n, :], OP.mult)
                                return e.tensor_tensor_scan(
                                    h32_r[:, hs, :], dA_r[:, da, :], dBu_sb[:],
                                    carry[:, col:col + 1], OP.mult, OP.add)
                            scv = b.op("vector", sct, s_v, upto=pmax(
                                dav_pt, bc_pt))
                            P[("Ssc", das)] = b.pt()
                            sct_pt = b.pt()
                            L[("dA", das)] = scv
                            if dbg and tile == 0 and db == 0 and n == 1:
                                b.wait("sync", s_v, scv)
                                b.dma("sync", dbg_sc[3], h32_r[:, hs, :], s_ld)
                                b.wait("vector", s_ld, s_ld.n)
                            if n % 2 == 1:
                                b.op("vector", lambda e, db=db, n0=n - 1: e.tensor_copy(
                                    carry[:, db * DN + n0:db * DN + n0 + 2],
                                    h32_r[:, :, LC - 1:LC]),
                                    s_v, upto=sct_pt)
                    L[("u", usl)] = s_v.n
                    if own:
                        t1s = db % 2
                        b.op("scalar", lambda e, ts=t1s: e.copy(
                            t1y_r[:, ts, :], PSE[:, 0:LC]), s_s,
                            upto=pmax(P.get(("Sy", 1)),
                                      P.get(("Std", t1s))))
                        P["St1c"] = b.pt()
                        t1c_pt = b.pt()
                        t1v = b.op("vector", lambda e, db=db, ts=t1s:
                                   e.scalar_tensor_tensor(
                                       t1y_r[:, ts, :], xc_sb[:, db, :],
                                       D_sb[:, db:db + 1], t1y_r[:, ts, :],
                                       OP.mult, OP.add), s_v, upto=t1c_pt)
                        L["pse"] = t1v
                        L[("t1y", t1s)] = b.dma("sync", sp_t1y[ot, db],
                                                t1y_r[:, t1s, :], s_t1s[t1s])
                        P[("Std", t1s)] = b.pt()
                L["scanv"] = s_v.n

        b.wait("vector", s_st, s_st.n)
        b.wait("vector", s_s, s_s.n)
        b.wait("vector", s_t, s_t.n)
        barv = b.op("vector", lambda e: e.tensor_copy(lnb[:, 10:11], lnb[:, 9:10]), s_v)
        for eng in ("scalar", "tensor", "sync", "gpsimd"):
            b.wait(eng, s_v, barv)
        mo_r = Cb_sb[:, 0:2, :]
        sz_r = Cb_sb[:, 2:4, :]
        t1yi = Cb_sb[:, 4:6, :]
        dwT_sb = Cb_sb[0:8, 6, :]
        dwb4 = Cb_sb[:, 7, 0:8]
        dwb_sb = Cb_sb[:, 8:16, :]
        hd_r = hbf_r[:, 0:2, :]
        gtf = gt_sb[:]
        xo_r = [x_rg[:, 0, 0:512].rearrange("p (a c) -> p a c", a=4),
                x_rg[:, 0, 512:1024].rearrange("p (a c) -> p a c", a=4)]
        ob_r = [x_rg[:, 1, 0:512].rearrange("p (a c) -> p a c", a=4),
                x_rg[:, 1, 512:1024].rearrange("p (a c) -> p a c", a=4)]

        def xo_t(s, tb):
            return x_rg[:, 0, s * 512 + tb * 128:s * 512 + (tb + 1) * 128]

        def ob_t(s, tb):
            return x_rg[:, 1, s * 512 + tb * 128:s * 512 + (tb + 1) * 128]

        for ot in range(nown):
            b.wait("sync", s_xn1s, s_xn1s.n)
            b.wait("sync", s_t, L.get("xnT_rd", 0))
            b.wait("sync", s_s, L.get("xnT_wr", 0))
            for g in range(16):
                b.dma("sync", xnT[:, g, :], sp_xn1[ot, g], s_xn1l)
            ld_xn1 = s_xn1l.n
            xl_pt = b.pt()
            for mblk in range(NDB):
                ws = mblk % 2
                b.dma("gpsimd", wbufW[:, ws * 16:(ws + 1) * 16, :], wzT[mblk],
                      s_w[ws], upto=pmax(P.get(("Awrd", ws)),
                                         P.get(("Zwr", ws)),
                                         P.get(("Ewr", ws)),
                                         P.get(("Owr", ws))))
                zld_pt = b.pt()
                ps = PSA if mblk % 2 == 0 else PSB

                def mmz(e, ws=ws, ps=ps):
                    ins = None
                    for kc in range(16):
                        ins = e.matmul(ps[:, 0:LC], wbufW[:, ws * 16 + kc, :],
                                       xnT[:, kc, :],
                                       start=(kc == 0), stop=(kc == 15))
                    return ins
                tv = b.op("tensor", mmz, s_t, upto=pmax(
                    zld_pt, xl_pt, P.get(("Zev", mblk % 2))))
                P[("Zwr", ws)] = b.pt()
                zmm_pt = b.pt()
                szv = b.op("scalar", lambda e, sl=mblk % 2, ps=ps: e.activation(
                    sz_r[:, sl, :], ps[:, 0:LC], AF.Silu, bias=0.0, scale=1.0),
                    s_s, upto=pmax(zmm_pt, P.get(("Zsz", mblk % 2)),
                                   P.get(("Esz", mblk % 2))))
                P[("Zev", mblk % 2)] = b.pt()
                zsz_pt = b.pt()
                tsl = mblk % 2
                b.dma("sync", t1yi[:, tsl, :], sp_t1y[ot, mblk], s_ti[tsl],
                      upto=P.get(("Zt1", tsl)))
                zt1_pt = b.pt()
                ygv = b.op("vector", lambda e, sl=mblk % 2, db=mblk:
                           e.tensor_tensor(xc_sb[:, db, :], t1yi[:, sl, :],
                                           sz_r[:, sl, :], OP.mult), s_v,
                           upto=pmax(zsz_pt, zt1_pt, P.get("Oxc")))
                P[("Zsz", mblk % 2)] = b.pt()
                P[("Zt1", tsl)] = b.pt()
            yg_done = s_v.n
            yg_pt = b.pt()

            for dmb in range(NKD):
                ws = dmb % 2
                wupto = pmax(P.get(("Awrd", ws)), P.get(("Zwr", ws)),
                             P.get(("Ewr", ws)), P.get(("Owr", ws)))
                b.dma("gpsimd", wbufW[:, 32 + ws * 16:32 + (ws + 1) * 16, :],
                      woT[dmb, :, 0:16, :], s_w[ws], upto=wupto)
                b.dma("gpsimd", wbufW[:, ws * 16:(ws + 1) * 16, :],
                      woT[dmb, :, 16:32, :], s_w[ws], upto=wupto)
                old_pt = b.pt()
                xsl = dmb % 2
                b.dma("sync", xo_r[xsl],
                      xin[(npre + ot) * LC:(npre + ot + 1) * LC,
                          dmb * 128:(dmb + 1) * 128]
                      .rearrange("(a p) c -> p a c", p=128), s_xo[xsl],
                      upto=P.get(("Oxo", xsl)))
                oxo_pt = b.pt()
                ps = PSA if dmb % 2 == 0 else PSB

                def mmo(e, ws=ws, ps=ps):
                    ins = None
                    for kc in range(32):
                        w = (wbufW[:, 32 + ws * 16 + kc, :] if kc < 16
                             else wbufW[:, ws * 16 + kc - 16, :])
                        ins = e.matmul(ps[:, 0:LC], w, xc_sb[:, kc, :],
                                       start=(kc == 0), stop=(kc == 31))
                    return ins
                tv = b.op("tensor", mmo, s_t, upto=pmax(
                    old_pt, yg_pt, P.get(("Oev", dmb % 2))))
                P[("Owr", ws)] = b.pt()
                omm_pt = b.pt()
                mev = b.op("scalar", lambda e, sl=dmb % 2, ps=ps: e.copy(
                    mo_r[:, sl, :], ps[:, 0:LC]), s_s, upto=pmax(
                    omm_pt, P.get(("Omo", dmb % 2)),
                    P.get(("Emo", dmb % 2))))
                P[("Oev", dmb % 2)] = b.pt()
                omev_pt = b.pt()
                pst = PST1 if dmb % 2 == 0 else PST2

                def trm(e, sl=dmb % 2, pst=pst):
                    ins = None
                    for j in range(4):
                        ins = e.matmul(pst[:, j * 128:(j + 1) * 128],
                                       mo_r[:, sl, j * 128:(j + 1) * 128],
                                       ident[:], start=True, stop=True)
                    return ins
                trv = b.op("tensor", trm, s_t, upto=pmax(
                    omev_pt, P.get(("Opst", dmb % 2))))
                P[("Omo", dmb % 2)] = b.pt()
                otr_pt = b.pt()
                osl = dmb % 2
                obe = b.op("scalar", lambda e, pst=pst, os=osl: e.copy(
                    x_rg[:, 1, os * 512:(os + 1) * 512], pst[:, 0:512]), s_s,
                    upto=pmax(otr_pt, P.get(("Oob", osl))))
                oob_pt = b.pt()

                def addm(e, xs=xsl, os=osl):
                    ins = None
                    for tb in range(4):
                        ins = e.tensor_tensor(
                            ob_t(os, tb), xo_t(xs, tb),
                            ob_t(os, tb), OP.add)
                    return ins
                av = b.op("vector", addm, s_v, upto=pmax(oob_pt, oxo_pt))
                P[("Opst", dmb % 2)] = b.pt()
                P[("Oxo", xsl)] = b.pt()
                oav_pt = b.pt()
                L[("obB", osl)] = b.dma(
                    "sync",
                    sp_x2[ot, :, :, dmb * 128:(dmb + 1) * 128]
                    .rearrange("a p c -> p a c"),
                    ob_r[osl], s_ob[osl], upto=oav_pt)
                P[("Oob", osl)] = b.pt()
            P["Oxc"] = b.pt()
            b.wait("sync", s_ob[0], s_ob[0].n)
            b.wait("sync", s_ob[1], s_ob[1].n)

            for tb in range(4):
                sl = tb % 2
                k = L.get(("xrg", sl))
                if k is not None:
                    b.wait("sync", s_v, k)
                b.dma("sync", x_rg[:, sl, :], sp_x2[ot, tb], s_x[sl])
                ldv = s_x[sl].n
                b.wait("scalar", s_x[sl], ldv)
                k = L.get("sq_v")
                if k is not None:
                    b.wait("scalar", s_v, k)
                k = L.get(("xn", sl))
                if k is not None:
                    b.wait("scalar", s_t, k)
                sqv = b.op("scalar", lambda e, sl=sl: e.activation(
                    xn_bf[:, sl, :], x_rg[:, sl, :], AF.Square, bias=0.0,
                    scale=1.0, accum_out=lnb[:, 1:2]), s_s)
                b.wait("vector", s_x[sl], ldv)
                b.wait("vector", s_s, sqv)

                b.op("vector", lambda e, sl=sl: e.reduce_sum(
                    lnb[:, 0:1], x_rg[:, sl, :], axis=AX), s_v)
                b.op("vector", lambda e: e.tensor_scalar_mul(
                    lnb[:, 2:3], lnb[:, 0:1], 1.0 / DM), s_v)
                b.op("vector", lambda e: e.tensor_scalar_mul(
                    lnb[:, 3:4], lnb[:, 1:2], 1.0 / DM), s_v)
                b.op("vector", lambda e: e.tensor_tensor(
                    lnb[:, 4:5], lnb[:, 2:3], lnb[:, 2:3], OP.mult), s_v)
                lv = b.op("vector", lambda e: e.tensor_tensor(
                    lnb[:, 5:6], lnb[:, 3:4], lnb[:, 4:5], OP.subtract), s_v)
                L["sq_v"] = lv

                def rstd(e):
                    e.activation(lnb[:, 6:7], lnb[:, 5:6], AF.Ln,
                                 bias=epsc[:, 0:1], scale=1.0)
                    return e.activation(lnb[:, 7:8], lnb[:, 6:7], AF.Exp,
                                        bias=0.0, scale=-0.5)
                b.wait("scalar", s_v, lv)
                rsv = b.op("scalar", rstd, s_s)
                b.wait("vector", s_s, rsv)
                k = L.get(("xn", sl))
                if k is not None:
                    b.wait("vector", s_t, k)
                xnv = b.op("vector", lambda e, sl=sl: e.tensor_scalar(
                    xn_bf[:, sl, :], x_rg[:, sl, :], lnb[:, 2:3], lnb[:, 7:8],
                    OP.subtract, OP.mult), s_v)
                L[("xrg", sl)] = xnv
                xnv2_pt = b.pt()
                for g in range(4):
                    pst = PST1 if g % 2 == 0 else PST2

                    def tr(e, g=g, sl=sl, pst=pst):
                        ins = None
                        for j in range(4):
                            dc = g * 4 + j
                            ins = e.matmul(pst[:, j * 128:(j + 1) * 128],
                                           xn_bf[:, sl, dc * 128:(dc + 1) * 128],
                                           ident[:], start=True, stop=True)
                        return ins
                    tv = b.op("tensor", tr, s_t, upto=pmax(
                        xnv2_pt, P.get(("LNpst", g % 2))))
                    L[("xn", sl)] = tv

                    def ev(e, g=g, tb=tb, pst=pst):
                        return e.copy(
                            xnT[:, g * 4:(g + 1) * 4, tb * 128:tb * 128 + 128],
                            pst[:, 0:512].rearrange("p (a c) -> p a c", a=4))
                    evv = b.op("scalar", ev, s_s)
                    P[("LNpst", g % 2)] = b.pt()
                    L[("pst", g % 2)] = evv
                    L["xnT_wr"] = evv
            xn2_done = s_s.n
            xn2_pt = b.pt()

            b.wait("tensor", s_s, xn2_done)
            k = L.get("psd")
            if k is not None:
                b.wait("tensor", s_s, k)
            k = L.get("pse_g")
            if k is not None:
                b.wait("tensor", s_v, k)

            def lgm(e):
                ins = None
                for kc in range(16):
                    ins = e.matmul(PSD[0:8, 0:LC], gwT_sb[:, kc, :],
                                   xnT[:, kc, :], start=(kc == 0),
                                   stop=(kc == 15))
                return ins
            tvl = b.op("tensor", lgm, s_t)
            b.wait("scalar", s_t, tvl)
            lgv = b.op("scalar", lambda e: e.copy(spt[0:8, 0:LC], PSD[0:8, 0:LC]),
                       s_s)
            L["psd"] = lgv
            for tb in range(4):
                k = L.get("dwb4")
                if k is not None:
                    b.wait("tensor", s_v, k)
                b.wait("tensor", s_s, lgv)
                k = L.get("pse_v")
                if k is not None:
                    b.wait("tensor", s_v, k)
                ttr = b.op("tensor", lambda e, tb=tb: e.matmul(
                    PSE[:, 0:8], spt[0:8, tb * 128:tb * 128 + 128],
                    ident_f32[0:8, 0:8], start=True, stop=True), s_t)
                if dbg and ot == 0:
                    b.op("scalar", lambda e, tb=tb: e.copy(
                        wbufA[:, 1, tb, 0:8], PSE[:, 0:8]), s_s)
                b.wait("vector", s_t, ttr)

                pev = b.op("scalar", lambda e: e.copy(
                    gtf[:, 120:128], PSE[:, 0:8]), s_s)
                b.wait("vector", s_s, pev)

                b.op("vector", lambda e: e.reduce_max(
                    gtf[:, 0:1], gtf[:, 120:128], axis=AX), s_v)
                g1 = b.op("vector", lambda e: e.tensor_scalar_mul(
                    gtf[:, 1:2], gtf[:, 0:1], -1.0), s_v)
                L["pse_v"] = g1
                b.wait("scalar", s_v, g1)
                g2 = b.op("scalar", lambda e: e.activation(
                    gtf[:, 8:16], gtf[:, 120:128], AF.Exp, bias=gtf[:, 1:2],
                    scale=1.0), s_s)
                b.wait("vector", s_s, g2)
                gate3_steps = [
                    lambda e: e.reduce_sum(gtf[:, 2:3], gtf[:, 8:16], axis=AX),
                    lambda e: e.reciprocal(gtf[:, 3:4], gtf[:, 2:3]),
                    lambda e: e.tensor_scalar(gtf[:, 16:24], gtf[:, 8:16],
                                              gtf[:, 3:4], None, OP.mult),
                    lambda e: e.tensor_tensor(gtf[:, 24:32], gtf[:, 16:24],
                                              gb_bc[:], OP.add),
                    lambda e: e.reduce_max(gtf[:, 4:5], gtf[:, 24:32], axis=AX),
                    lambda e: e.tensor_scalar(gtf[:, 32:40], gtf[:, 24:32],
                                              gtf[:, 4:5], None, OP.is_equal),
                    lambda e: e.tensor_tensor_scan(gtf[:, 40:48], ones8[:],
                                                   gtf[:, 32:40], 0.0,
                                                   OP.mult, OP.add),
                    lambda e: e.tensor_scalar(gtf[:, 48:56], gtf[:, 40:48],
                                              1.0, None, OP.is_equal),
                    lambda e: e.tensor_tensor(gtf[:, 56:64], gtf[:, 32:40],
                                              gtf[:, 48:56], OP.mult),
                    lambda e: e.scalar_tensor_tensor(gtf[:, 64:72],
                                                     gtf[:, 56:64],
                                                     m30[:, 0:1], gtf[:, 24:32],
                                                     OP.mult, OP.add),
                    lambda e: e.reduce_max(gtf[:, 5:6], gtf[:, 64:72], axis=AX),
                    lambda e: e.tensor_scalar(gtf[:, 72:80], gtf[:, 64:72],
                                              gtf[:, 5:6], None, OP.is_equal),
                    lambda e: e.tensor_tensor_scan(gtf[:, 80:88], ones8[:],
                                                   gtf[:, 72:80], 0.0,
                                                   OP.mult, OP.add),
                    lambda e: e.tensor_scalar(gtf[:, 88:96], gtf[:, 80:88],
                                              1.0, None, OP.is_equal),
                    lambda e: e.tensor_tensor(gtf[:, 96:104], gtf[:, 72:80],
                                              gtf[:, 88:96], OP.mult),
                    lambda e: e.tensor_tensor(gtf[:, 104:112], gtf[:, 56:64],
                                              gtf[:, 96:104], OP.add),
                    lambda e: e.tensor_tensor(gtf[:, 112:120], gtf[:, 16:24],
                                              gtf[:, 104:112], OP.mult),
                    lambda e: e.tensor_copy(dwb4, gtf[:, 112:120]),
                ]
                for st in gate3_steps:
                    g3 = b.op("vector", st, s_v)
                if dbg and ot == 0:
                    b.op("vector", lambda e, tb=tb: e.tensor_copy(
                        wbufA[:, 0, tb, 0:120], gtf[:, 0:120]), s_v)
                    b.wait("sync", s_v, g3)
                    gdv = b.dma("sync", dbg_gt[tb], gtf[:, 0:120], s_ld)
                    b.wait("vector", s_ld, gdv)
                b.wait("tensor", s_v, g3)
                ttv = b.op("tensor", lambda e: e.matmul(
                    PST1[0:8, 0:128], dwb4, ident[:], start=True, stop=True), s_t)
                L["dwb4"] = ttv
                b.wait("scalar", s_t, ttv)
                dtv = b.op("scalar", lambda e, tb=tb: e.copy(
                    dwT_sb[:, tb * 128:tb * 128 + 128], PST1[0:8, 0:128]), s_s)
                L["dwt"] = dtv
                b.wait("tensor", s_s, dtv)
            if dbg and ot == 0:
                b.dma("sync", dbg_snapg, wbufA[:, 0], s_ld)
                b.dma("sync", dbg_snapP, wbufA[:, 1], s_ld)
            for en in range(8):
                if en == 0:
                    k = L.get("psd")
                    if k is not None:
                        b.wait("tensor", s_s, k)
                    b.wait("tensor", s_s, L["dwt"])

                def dbm(e, en=en):
                    return e.matmul(PSD[:, 0:LC], sel8[:, en, :], dwT_sb[:],
                                    start=True, stop=True)
                tv = b.op("tensor", dbm, s_t)
                b.wait("scalar", s_t, tv)
                dev = b.op("scalar", lambda e, en=en: e.copy(
                    dwb_sb[:, en, :], PSD[:, 0:LC]), s_s)
                b.wait("tensor", s_s, dev)
            dwb_done = s_s.n
            dwb_pt = b.pt()
            if dbg and ot == 0:
                b.wait("sync", s_s, dwb_done)
                b.dma("sync", dbg_dw, dwT_sb[:], s_ld)
                b.dma("sync", dbg_dwb, dwb_sb[:], s_ld)
                b.wait("scalar", s_ld, s_ld.n)

            for en in range(9):
                for ib in range(8):
                    ws = ib % 2
                    src1 = w1T[en, ib] if en < 8 else ws1T[ib]
                    src3 = w3T[en, ib] if en < 8 else ws3T[ib]
                    eupto = pmax(P.get(("Ewr", ws)), P.get(("Owr", ws)),
                                 P.get(("Zwr", ws)), P.get(("Awrd", ws)))
                    b.dma("gpsimd", wbufW[:, ws * 16:(ws + 1) * 16, :], src1,
                          s_w[ws], upto=eupto)
                    b.dma("gpsimd", wbufW[:, 32 + ws * 16:32 + (ws + 1) * 16, :],
                          src3, s_w[ws], upto=eupto)
                    wld_pt = b.pt()

                    def mm13(e, ws=ws):
                        for kc in range(16):
                            e.matmul(PSA[:, 0:LC], wbufW[:, ws * 16 + kc, :],
                                     xnT[:, kc, :],
                                     start=(kc == 0), stop=(kc == 15))
                        ins = None
                        for kc in range(16):
                            ins = e.matmul(PSB[:, 0:LC],
                                           wbufW[:, 32 + ws * 16 + kc, :],
                                           xnT[:, kc, :],
                                           start=(kc == 0), stop=(kc == 15))
                        return ins
                    tv = b.op("tensor", mm13, s_t, upto=pmax(
                        wld_pt, xn2_pt, P.get(("Eev", ws))))
                    P[("Ewr", ws)] = b.pt()
                    mm_pt = b.pt()
                    s1v = b.op("scalar", lambda e, sl=ib % 2: e.activation(
                        sz_r[:, sl, :], PSA[:, 0:LC], AF.Silu, bias=0.0,
                        scale=1.0), s_s, upto=pmax(
                        mm_pt, P.get(("Esz", ib % 2))))
                    w3e = b.op("scalar", lambda e, sl=ib % 2: e.copy(
                        mo_r[:, sl, :], PSB[:, 0:LC]), s_s, upto=pmax(
                        mm_pt, P.get(("Emo", ib % 2))))
                    P[("Eev", ws)] = b.pt()
                    w3e_pt = b.pt()
                    hsl = ib % 2

                    b.op("vector", lambda e, sl=ib % 2: e.tensor_tensor(
                        mo_r[:, sl, :], sz_r[:, sl, :],
                        mo_r[:, sl, :], OP.mult), s_v, upto=pmax(
                        w3e_pt, dwb_pt))
                    P[("Esz", ib % 2)] = b.pt()
                    hm1_pt = b.pt()

                    def hmul(e, sl=ib % 2, en=en, hs=hsl):
                        if en < 8:
                            return e.tensor_tensor(hd_r[:, hs, :],
                                                   mo_r[:, sl, :],
                                                   dwb_sb[:, en, :], OP.mult)
                        return e.tensor_copy(hd_r[:, hs, :], mo_r[:, sl, :])
                    hv = b.op("vector", hmul, s_v, upto=pmax(
                        hm1_pt, dwb_pt, P.get(("Ehd", hsl))))
                    P[("Emo", ib % 2)] = b.pt()
                    hv_pt = b.pt()
                    L[("hd", hsl)] = b.dma("sync", sp_hc[ot, en * 8 + ib],
                                           hd_r[:, hsl, :], s_hd[hsl],
                                           upto=hv_pt)
                    P[("Ehd", hsl)] = b.pt()

        b.wait("vector", s_s, s_s.n)
        b.wait("vector", s_t, s_t.n)
        barv2 = b.op("vector", lambda e: e.tensor_copy(lnb[:, 11:12],
                                                       lnb[:, 10:11]), s_v)
        for eng in ("scalar", "tensor", "sync", "gpsimd"):
            b.wait(eng, s_v, barv2)

        def hslot(j):
            if j < 32:
                return xc_sb[:, j, :]
            if j < 48:
                return Bb_sb[:, j - 32, :]
            if j < 64:
                return xnT[:, j - 48, :]
            if j < 68:
                return hbf_r[:, j - 64, :]
            if j < 70:
                return tmp_r[:, j - 68, :]
            return u_r[:, j - 70, :]

        b.wait("sync", s_hd[0], s_hd[0].n)
        b.wait("sync", s_hd[1], s_hd[1].n)
        for ot in range(nown):
            for j in range(NKC):
                b.dma("sync", hslot(j), sp_hc[ot, j], s_hcl,
                      upto=P.get("Dmm"))
            ld_h = s_hcl.n
            h_pt = b.pt()
            for dmb in range(NKD):
                wupto = pmax(P.get("Dwr"), h_pt)
                b.dma("gpsimd", wbufW[:, 0:36, :],
                      w2T[dmb, :, 0:NKC // 2, :], s_w[dmb % 2], upto=wupto)
                b.dma("gpsimd", wbufW[:, 36:72, :],
                      w2T[dmb, :, NKC // 2:NKC, :], s_w[dmb % 2], upto=wupto)
                dld_pt = b.pt()
                ps = PSA if dmb % 2 == 0 else PSB

                def mmd(e, ps=ps):
                    ins = None
                    for kc in range(NKC):
                        ins = e.matmul(ps[:, 0:LC], wbufW[:, kc, :], hslot(kc),
                                       start=(kc == 0), stop=(kc == NKC - 1))
                    return ins
                tv = b.op("tensor", mmd, s_t, upto=pmax(
                    dld_pt, h_pt, P.get(("Dev", dmb % 2))))
                P["Dwr"] = b.pt()
                P["Dmm"] = b.pt()
                dmm_pt = b.pt()
                mev = b.op("scalar", lambda e, sl=dmb % 2, ps=ps: e.copy(
                    mo_r[:, sl, :], ps[:, 0:LC]), s_s, upto=pmax(
                    dmm_pt, P.get(("Dmo", dmb % 2))))
                P[("Dev", dmb % 2)] = b.pt()
                dmev_pt = b.pt()
                pst = PST1 if dmb % 2 == 0 else PST2

                def trd(e, sl=dmb % 2, pst=pst):
                    ins = None
                    for j in range(4):
                        ins = e.matmul(pst[:, j * 128:(j + 1) * 128],
                                       mo_r[:, sl, j * 128:(j + 1) * 128],
                                       ident[:], start=True, stop=True)
                    return ins
                trv = b.op("tensor", trd, s_t, upto=pmax(
                    dmev_pt, P.get(("Dpst", dmb % 2))))
                P[("Dmo", dmb % 2)] = b.pt()
                dtr_pt = b.pt()
                xsl = dmb % 2
                b.dma("sync", xo_r[xsl],
                      sp_x2[ot, :, :, dmb * 128:(dmb + 1) * 128]
                      .rearrange("a p c -> p a c"), s_xo[xsl],
                      upto=P.get(("Dxo", xsl)))
                dxo_pt = b.pt()
                osl = dmb % 2
                obe = b.op("scalar", lambda e, pst=pst, os=osl: e.copy(
                    x_rg[:, 1, os * 512:(os + 1) * 512], pst[:, 0:512]), s_s,
                    upto=pmax(dtr_pt, P.get(("Dob", osl))))
                dob_pt = b.pt()

                def addo(e, xs=xsl, os=osl):
                    ins = None
                    for tb in range(4):
                        ins = e.tensor_tensor(
                            ob_t(os, tb), xo_t(xs, tb),
                            ob_t(os, tb), OP.add)
                    return ins
                ov = b.op("vector", addo, s_v, upto=pmax(dob_pt, dxo_pt))
                P[("Dpst", dmb % 2)] = b.pt()
                P[("Dxo", xsl)] = b.pt()
                dov_pt = b.pt()
                L[("ob", osl)] = b.dma(
                    "sync",
                    yout[ot * LC:(ot + 1) * LC, dmb * 128:(dmb + 1) * 128]
                    .rearrange("(a p) c -> p a c", p=128),
                    ob_r[osl], s_ob[osl], upto=dov_pt)
                P[("Dob", osl)] = b.pt()


        if SERIAL:
            def _final(e, cnts=tuple(b.ecnt), sems=list(b.esems)):
                for i in range(5):
                    if cnts[i] > 0:
                        e.wait_ge(sems[i].h, cnts[i])
            b.q["gpsimd"].append(_final)
        else:
            b.wait("gpsimd", s_ob[0], s_ob[0].n)
            b.wait("gpsimd", s_ob[1], s_ob[1].n)
            b.wait("gpsimd", s_ld, s_ld.n)
            b.wait("gpsimd", s_w0, s_w0.n)
            b.wait("gpsimd", s_w1, s_w1.n)

        with nc.Block() as block:
            @block.sync
            def _(e):
                for fn in b.q["sync"]:
                    fn(e)

            @block.gpsimd
            def _(e):
                for fn in b.q["gpsimd"]:
                    fn(e)

            @block.vector
            def _(e):
                for fn in b.q["vector"]:
                    fn(e)

            @block.scalar
            def _(e):
                for fn in b.q["scalar"]:
                    fn(e)

            @block.tensor
            def _(e):
                for fn in b.q["tensor"]:
                    fn(e)

    nc._gtot = b.gtot
    return nc


def _tile_lhsT(w, nm, nk):
    M, K = np.asarray(w).shape
    assert M == nm * 128 and K == nk * 128
    return np.ascontiguousarray(
        np.asarray(w).reshape(nm, 128, nk, 128).transpose(0, 3, 2, 1))


def _per_partition(v, cols):
    a = np.asarray(v)
    if a.ndim == 1:
        a = a.reshape(NDB, 128).transpose(1, 0)
        return np.ascontiguousarray(a)
    a = a.reshape(NDB, 128, cols).transpose(1, 0, 2).reshape(128, NDB * cols)
    return np.ascontiguousarray(a)


def stage_inputs(inputs):
    ip = np.asarray(inputs["L0_attn_in_proj_weight"])
    common = dict(
        wxT=_tile_lhsT(ip[:DI], NDB, 16),
        wzT=_tile_lhsT(ip[DI:], NDB, 16),
        xpT=np.ascontiguousarray(
            np.asarray(inputs["L0_attn_x_proj_weight"]).T
            .reshape(NDB, 128, 160).transpose(1, 0, 2)),
        dtTw=np.ascontiguousarray(
            np.asarray(inputs["L0_attn_dt_proj_weight"]).T
            .reshape(128, NDB, 128)),
        woT=_tile_lhsT(np.asarray(inputs["L0_attn_out_proj_weight"]), NKD, 32),
        w1T=np.stack([_tile_lhsT(np.asarray(inputs["L0_ffn_W1"])[e], 8, 16)
                      for e in range(NE)]),
        w3T=np.stack([_tile_lhsT(np.asarray(inputs["L0_ffn_W3"])[e], 8, 16)
                      for e in range(NE)]),
        w2T=_tile_lhsT(
            np.concatenate([np.asarray(inputs["L0_ffn_W2"])[e] for e in range(NE)]
                           + [np.asarray(inputs["L0_ffn_Ws2"])], axis=1),
            NKD, NKC),
        ws1T=_tile_lhsT(np.asarray(inputs["L0_ffn_Ws1"]), 8, 16),
        ws3T=_tile_lhsT(np.asarray(inputs["L0_ffn_Ws3"]), 8, 16),
        gwTi=np.ascontiguousarray(
            np.asarray(inputs["L0_ffn_gate_w"]).T
            .reshape(16, 128, NE).transpose(1, 0, 2)),
        alog=_per_partition(np.asarray(inputs["L0_attn_A_log"]), DN),
        dtbi=_per_partition(np.asarray(inputs["L0_attn_dt_proj_bias"]), 1),
        dvec=_per_partition(np.asarray(inputs["L0_attn_D"]), 1),
        cwin=_per_partition(np.asarray(inputs["L0_attn_conv1d_weight"]), DC),
        gbin=np.ascontiguousarray(np.asarray(inputs["L0_ffn_gate_b"]).reshape(1, NE)),
    )
    bf_keys = {"wxT", "wzT", "xpT", "dtTw", "woT", "w1T", "w3T", "w2T",
               "ws1T", "ws3T", "gwTi"}
    return {k: np.ascontiguousarray(
                v.astype(ml_dtypes.bfloat16 if k in bf_keys else np.float32))
            for k, v in common.items()}


def kernel(**inputs):
    x = np.asarray(inputs["L0_attn_hidden_states"])
    common = stage_inputs(inputs)
    npre, nown = NPRE_DEF, NOWN_DEF
    in_maps = []
    for core in range(8):
        bidx, q = core // 4, core % 4
        own_end = (q + 1) * nown * LC
        pre = q * nown * LC
        xin = np.zeros((npre + nown) * LC * DM, dtype=np.float32).reshape(-1, DM)
        xin[(npre + nown) * LC - own_end:, :] = x[bidx, :own_end, :]
        m = dict(common)
        m["xin"] = np.ascontiguousarray(xin)
        in_maps.append(m)
    nc = build(npre, nown)
    res = run_bass_kernel_spmd(nc, in_maps, core_ids=list(range(8)))
    shards = [res.results[c]["yout"] for c in range(8)]
    out = np.stack(shards).reshape(NB, 4, nown * LC, DM).reshape(NB, LSEQ, DM)
    return out.astype(np.float32)


# version 10 (final)
# speedup vs baseline: 1.1830x; 1.1830x over previous
import contextlib

import ml_dtypes
import numpy as np

import concourse.bass as bass
import concourse.mybir as mybir
from concourse.bass_utils import run_bass_kernel_spmd

F32 = mybir.dt.float32
BF16 = mybir.dt.bfloat16
AF = mybir.ActivationFunctionType
OP = mybir.AluOpType
AX = mybir.AxisListType.X

DM, DI, DN, DR, DC = 2048, 4096, 16, 128, 4
NE, NI = 8, 1024
LSEQ, NB = 8192, 2
LC = 512
NDB = DI // 128
NKD = DM // 128
KCAT = NE * NI + NI
NKC = KCAT // 128
EPS = 1e-5
NPRE_DEF, NOWN_DEF = 12, 4


class Sem:
    def __init__(self, h):
        self.h = h
        self.n = 0


def pmax(*pts):
    pts = [p for p in pts if p is not None]
    if not pts:
        return None
    return tuple(max(p[i] for p in pts) for i in range(5))


SERIAL = True


class Builder:

    ENGS = ("sync", "gpsimd", "vector", "scalar", "tensor")
    EID = {e: i for i, e in enumerate(ENGS)}

    def __init__(self):
        self.q = {e: [] for e in self.ENGS}
        self.last = {}
        self.esems = None
        self.ecnt = [0, 0, 0, 0, 0]
        self.hw = [[0] * 5 for _ in range(5)]
        self.gtot = 0
        self.vfence = None

    def pt(self):
        return tuple(self.ecnt)

    def _emit_waits(self, eid, tgt):
        ws = []
        for y in range(5):
            if y == eid and eid != 2:
                continue
            v = tgt[y]
            if v > self.hw[eid][y]:
                ws.append((self.esems[y], v))
                self.hw[eid][y] = v
        return ws

    def wait(self, eng, sem, val):
        if SERIAL:
            return
        if val is None or val <= 0:
            return
        self.q[eng].append(lambda e, s=sem.h, v=val: e.wait_ge(s, v))

    def op(self, eng, fn, sem=None, upto=None):
        v = None
        if sem is not None:
            sem.n += 1
            v = sem.n

        if SERIAL:
            eid = self.EID[eng]
            tgt = upto if upto is not None else self.pt()
            ws = self._emit_waits(eid, tgt)
            fence = (eng == "vector")
            me = self
            mysem = self.esems[eid]
            self.ecnt[eid] += 1
            self.gtot += 1

            def g(e, fn=fn, ws=ws, fence=fence, me=me, mysem=mysem):
                for s, val_ in ws:
                    e.wait_ge(s.h, val_)
                ins = fn(e)
                assert ins is not None, "op fn must return its last instruction"
                if fence:
                    ins = e.memset(me.vfence, 0.0)
                ins.then_inc(mysem.h, 1)
            self.q[eng].append(g)
            return v

        def g(e, fn=fn, sem=sem):
            ins = fn(e)
            if sem is not None:
                ins.then_inc(sem.h, 1)
        self.q[eng].append(g)
        return v

    def dma(self, eng, out, in_, sem, upto=None):
        sem.n += 16
        v = sem.n

        if SERIAL:
            eid = self.EID[eng]
            tgt = upto if upto is not None else self.pt()
            own_prev = self.ecnt[eid]
            ws = self._emit_waits(eid, tgt)
            mysem = self.esems[eid]
            self.ecnt[eid] += 16
            self.gtot += 16

            def g(e, out=out, in_=in_, ws=ws, mysem=mysem, own_prev=own_prev):
                for s, val_ in ws:
                    e.wait_ge(s.h, val_)
                if own_prev > 0:
                    e.wait_ge(mysem.h, own_prev)
                e.dma_start(out=out, in_=in_).then_inc(mysem.h, 16)
            self.q[eng].append(g)
            return v

        def g(e, out=out, in_=in_, sem=sem):
            e.dma_start(out=out, in_=in_).then_inc(sem.h, 16)
        self.q[eng].append(g)
        return v


def build(npre=NPRE_DEF, nown=NOWN_DEF, dbg=False):
    nt = npre + nown
    ntok = nt * LC
    nc = bass.Bass("TRN2", target_bir_lowering=False, debug=False,
                   detect_race_conditions=False)

    xin = nc.dram_tensor("xin", [ntok, DM], F32, kind="ExternalInput").ap()
    wxT = nc.dram_tensor("wxT", [NDB, 128, 16, 128], BF16, kind="ExternalInput").ap()
    wzT = nc.dram_tensor("wzT", [NDB, 128, 16, 128], BF16, kind="ExternalInput").ap()
    xpT = nc.dram_tensor("xpT", [128, NDB, 160], BF16, kind="ExternalInput").ap()
    dtTw = nc.dram_tensor("dtTw", [128, NDB, 128], BF16, kind="ExternalInput").ap()
    woT = nc.dram_tensor("woT", [NKD, 128, 32, 128], BF16, kind="ExternalInput").ap()
    w1T = nc.dram_tensor("w1T", [NE, 8, 128, 16, 128], BF16, kind="ExternalInput").ap()
    w3T = nc.dram_tensor("w3T", [NE, 8, 128, 16, 128], BF16, kind="ExternalInput").ap()
    w2T = nc.dram_tensor("w2T", [NKD, 128, NKC, 128], BF16, kind="ExternalInput").ap()
    ws1T = nc.dram_tensor("ws1T", [8, 128, 16, 128], BF16, kind="ExternalInput").ap()
    ws3T = nc.dram_tensor("ws3T", [8, 128, 16, 128], BF16, kind="ExternalInput").ap()
    gwTi = nc.dram_tensor("gwTi", [128, 16, NE], BF16, kind="ExternalInput").ap()
    alog = nc.dram_tensor("alog", [128, NDB * DN], F32, kind="ExternalInput").ap()
    dtbi = nc.dram_tensor("dtbi", [128, NDB], F32, kind="ExternalInput").ap()
    dvec = nc.dram_tensor("dvec", [128, NDB], F32, kind="ExternalInput").ap()
    cwin = nc.dram_tensor("cwin", [128, NDB * DC], F32, kind="ExternalInput").ap()
    gbin = nc.dram_tensor("gbin", [1, NE], F32, kind="ExternalInput").ap()
    yout = nc.dram_tensor("yout", [nown * LC, DM], F32, kind="ExternalOutput").ap()

    spk = "ExternalOutput" if dbg else "Internal"
    if dbg:
        dbg_xc = nc.dram_tensor("dbg_xc", [NDB, 128, LC], BF16, kind="ExternalOutput").ap()
        dbg_bc = nc.dram_tensor("dbg_bc", [32, LC], F32, kind="ExternalOutput").ap()
        dbg_dl = nc.dram_tensor("dbg_dl", [2, 128, LC], F32, kind="ExternalOutput").ap()
        dbg_da = nc.dram_tensor("dbg_da", [DN, 128, LC], F32, kind="ExternalOutput").ap()
        dbg_h = nc.dram_tensor("dbg_h", [DN, 128, LC], BF16, kind="ExternalOutput").ap()
        dbg_A = nc.dram_tensor("dbg_A", [128, NDB * DN], F32, kind="ExternalOutput").ap()
        dbg_cr = nc.dram_tensor("dbg_cr", [16, 128, NDB * DN], F32, kind="ExternalOutput").ap()
        dbg_dw = nc.dram_tensor("dbg_dw", [8, LC], BF16, kind="ExternalOutput").ap()
        dbg_dwb = nc.dram_tensor("dbg_dwb", [128, 8, LC], BF16, kind="ExternalOutput").ap()
        dbg_gt = nc.dram_tensor("dbg_gt", [4, 128, 120], F32, kind="ExternalOutput").ap()
        dbg_snapg = nc.dram_tensor("dbg_snapg", [128, 16, 128], BF16, kind="ExternalOutput").ap()
        dbg_snapP = nc.dram_tensor("dbg_snapP", [128, 16, 128], BF16, kind="ExternalOutput").ap()
        dbg_sc = nc.dram_tensor("dbg_sc", [4, 128, LC], F32, kind="ExternalOutput").ap()
    sp_xn1 = nc.dram_tensor("sp_xn1", [nown, 16, 128, LC], BF16, kind=spk).ap()
    sp_t1y = nc.dram_tensor("sp_t1y", [nown, NDB, 128, LC], BF16, kind=spk).ap()
    sp_x2 = nc.dram_tensor("sp_x2", [nown, 4, 128, DM], F32, kind=spk).ap()
    sp_hc = nc.dram_tensor("sp_hc", [nown, NKC, 128, LC], BF16, kind=spk).ap()

    b = Builder()
    ctx = contextlib.ExitStack()
    with ctx:
        b.esems = [Sem(ctx.enter_context(nc.semaphore(f"s_e{i}")))
                   for i in range(5)]
        s_ld = Sem(ctx.enter_context(nc.semaphore("s_ld")))
        s_st = Sem(ctx.enter_context(nc.semaphore("s_st")))
        s_v = Sem(ctx.enter_context(nc.semaphore("s_v")))
        s_s = Sem(ctx.enter_context(nc.semaphore("s_s")))
        s_t = Sem(ctx.enter_context(nc.semaphore("s_t")))
        s_g = Sem(ctx.enter_context(nc.semaphore("s_g")))
        s_w0 = Sem(ctx.enter_context(nc.semaphore("s_w0")))
        s_w1 = Sem(ctx.enter_context(nc.semaphore("s_w1")))
        s_w = [s_w0, s_w1]
        s_x = [Sem(ctx.enter_context(nc.semaphore("s_x0"))),
               Sem(ctx.enter_context(nc.semaphore("s_x1")))]
        s_t1s = [Sem(ctx.enter_context(nc.semaphore("s_t1s0"))),
                 Sem(ctx.enter_context(nc.semaphore("s_t1s1")))]
        s_hd = [Sem(ctx.enter_context(nc.semaphore("s_hd0"))),
                Sem(ctx.enter_context(nc.semaphore("s_hd1")))]
        s_x2s = Sem(ctx.enter_context(nc.semaphore("s_x2s")))
        s_xn1s = Sem(ctx.enter_context(nc.semaphore("s_xn1s")))
        s_xn1l = Sem(ctx.enter_context(nc.semaphore("s_xn1l")))
        s_ti = [Sem(ctx.enter_context(nc.semaphore("s_ti0"))),
                Sem(ctx.enter_context(nc.semaphore("s_ti1")))]
        s_xo = [Sem(ctx.enter_context(nc.semaphore("s_xo0"))),
                Sem(ctx.enter_context(nc.semaphore("s_xo1")))]
        s_hcl = Sem(ctx.enter_context(nc.semaphore("s_hcl")))
        s_ob = [Sem(ctx.enter_context(nc.semaphore("s_ob0"))),
                Sem(ctx.enter_context(nc.semaphore("s_ob1")))]

        vfence_t = ctx.enter_context(nc.sbuf_tensor("vfence", [128, 1], F32))
        b.vfence = vfence_t[:]
        ident = ctx.enter_context(nc.sbuf_tensor("ident", [128, 128], BF16))
        ident_f32 = ctx.enter_context(nc.sbuf_tensor("ident_f32", [128, 128], F32))
        ones1 = ctx.enter_context(nc.sbuf_tensor("ones1", [1, 128], BF16))
        ones8 = ctx.enter_context(nc.sbuf_tensor("ones8", [128, 8], F32))
        A_sb = ctx.enter_context(nc.sbuf_tensor("A_sb", [128, NDB * DN], F32))
        dtb_sb = ctx.enter_context(nc.sbuf_tensor("dtb_sb", [128, NDB], F32))
        D_sb = ctx.enter_context(nc.sbuf_tensor("D_sb", [128, NDB], F32))
        cw_sb = ctx.enter_context(nc.sbuf_tensor("cw_sb", [128, NDB * DC], F32))
        carry = ctx.enter_context(nc.sbuf_tensor("carry", [128, NDB * DN], F32))
        gb_bc = ctx.enter_context(nc.sbuf_tensor("gb_bc", [128, 8], F32))
        gbrow = ctx.enter_context(nc.sbuf_tensor("gbrow", [1, 8], BF16))
        xpT_sb = ctx.enter_context(nc.sbuf_tensor("xpT_sb", [128, NDB, 160], BF16))
        dtT_sb = ctx.enter_context(nc.sbuf_tensor("dtT_sb", [128, NDB, 128], BF16))
        gwT_sb = ctx.enter_context(nc.sbuf_tensor("gwT_sb", [128, 16, 8], BF16))
        halo = ctx.enter_context(nc.sbuf_tensor("halo", [128, NDB, 4], BF16))
        lnb = ctx.enter_context(nc.sbuf_tensor("lnb", [128, 16], F32))
        epsc = ctx.enter_context(nc.sbuf_tensor("epsc", [128, 1], F32))
        onec = ctx.enter_context(nc.sbuf_tensor("onec", [128, 1], F32))
        m30 = ctx.enter_context(nc.sbuf_tensor("m30", [128, 1], F32))
        xn_bf = ctx.enter_context(nc.sbuf_tensor("xn_bf", [128, 2, DM], BF16))
        xnT = ctx.enter_context(nc.sbuf_tensor("xnT", [128, 16, LC], BF16))
        wbufA = ctx.enter_context(nc.sbuf_tensor("wbufA", [128, 2, 16, 128], BF16))
        xc_sb = ctx.enter_context(nc.sbuf_tensor("xc_sb", [128, NDB, LC], BF16))
        Bb_sb = ctx.enter_context(nc.sbuf_tensor("Bb_sb", [128, 16, LC], BF16))
        Cb_sb = ctx.enter_context(nc.sbuf_tensor("Cb_sb", [128, 16, LC], BF16))
        sel32 = ctx.enter_context(nc.sbuf_tensor("sel32", [32, 32, 128], BF16))
        sel8 = ctx.enter_context(nc.sbuf_tensor("sel8", [8, 8, 128], BF16))

        PSA = ctx.enter_context(nc.psum_tensor("PSA", [128, 512], F32))
        PSB = ctx.enter_context(nc.psum_tensor("PSB", [128, 512], F32))
        PSD = ctx.enter_context(nc.psum_tensor("PSD", [128, 512], F32))
        PSE = ctx.enter_context(nc.psum_tensor("PSE", [128, 512], F32))
        PSJ = ctx.enter_context(nc.psum_tensor("PSJ", [128, 512], F32))
        PST1 = ctx.enter_context(nc.psum_tensor("PST1", [128, 512], F32))
        PST2 = ctx.enter_context(nc.psum_tensor("PST2", [128, 512], F32))

        L = b.last
        P = b.last

        def flush_pe(e):
            e.matmul(PSJ[:, 0:LC], ident[:], Bb_sb[:, 0, :], start=True, stop=True)
            return e.matmul(PSJ[:, 0:LC], ident[:], Bb_sb[:, 0, :], start=True,
                            stop=True)

        b.op("gpsimd", lambda e: e.memset(ident[:], 0.0), s_g)
        b.op("gpsimd", lambda e: e.memset(ident_f32[:], 0.0), s_g)
        b.op("gpsimd", lambda e: e.memset(ones1[:], 1.0), s_g)
        b.op("gpsimd", lambda e: e.memset(ones8[:], 1.0), s_g)
        b.op("gpsimd", lambda e: e.memset(halo[:], 0.0), s_g)
        b.op("gpsimd", lambda e: e.memset(carry[:], 0.0), s_g)
        b.op("gpsimd", lambda e: e.memset(epsc[:], EPS), s_g)
        b.op("gpsimd", lambda e: e.memset(onec[:], 1.0), s_g)
        b.op("gpsimd", lambda e: e.memset(m30[:], -1e30), s_g)
        b.op("gpsimd", lambda e: e.memset(lnb[:], 0.0), s_g)
        b.op("gpsimd", lambda e: e.memset(gt_sb[:], 123.0), s_g)
        b.op("gpsimd", lambda e: e.memset(sel32[:], 0.0), s_g)
        b.op("gpsimd", lambda e: e.memset(sel8[:], 0.0), s_g)
        b.wait("gpsimd", s_g, s_g.n)
        b.op("gpsimd", lambda e: e.affine_select(
            out=ident[:], in_=ident[:], compare_op=OP.not_equal, fill=1.0,
            base=0, pattern=[[-1, 128]], channel_multiplier=1), s_g)
        b.op("gpsimd", lambda e: e.affine_select(
            out=ident_f32[:], in_=ident_f32[:], compare_op=OP.not_equal, fill=1.0,
            base=0, pattern=[[-1, 128]], channel_multiplier=1), s_g)
        for n in range(32):
            b.op("gpsimd", lambda e, n=n: e.affine_select(
                out=sel32[:, n, :], in_=sel32[:, n, :],
                compare_op=OP.not_equal, fill=1.0, base=-n,
                pattern=[[0, 128]], channel_multiplier=1), s_g)
        for n in range(8):
            b.op("gpsimd", lambda e, n=n: e.affine_select(
                out=sel8[:, n, :], in_=sel8[:, n, :],
                compare_op=OP.not_equal, fill=1.0, base=-n,
                pattern=[[0, 128]], channel_multiplier=1), s_g)
        consts_done = s_g.n

        b.dma("sync", A_sb[:], alog, s_ld)
        b.dma("sync", dtb_sb[:], dtbi, s_ld)
        b.dma("sync", D_sb[:], dvec, s_ld)
        b.dma("sync", cw_sb[:], cwin, s_ld)
        b.dma("gpsimd", gbrow[0:1, :], gbin, s_w0)
        b.wait("gpsimd", s_w0, s_w0.n)
        b.wait("scalar", s_ld, s_ld.n)
        b.op("scalar", lambda e: e.activation(spt[:], A_sb[:], AF.Exp, bias=0.0, scale=1.0), s_s)
        b.wait("vector", s_s, s_s.n)
        b.op("vector", lambda e: e.tensor_scalar_mul(A_sb[:], spt[:], -1.0), s_v)
        b.wait("tensor", s_ld, s_ld.n)
        b.wait("tensor", s_w0, s_w0.n)
        tv = b.op("tensor", lambda e: e.matmul(PSD[:, 0:8], ones1[:], gbrow[0:1, :],
                                               start=True, stop=True), s_t)
        b.wait("scalar", s_t, tv)
        b.op("scalar", lambda e: e.copy(gb_bc[:], PSD[:, 0:8]), s_s)

        x_rg = ctx.enter_context(nc.sbuf_tensor("x_rg", [128, 2, DM], F32))
        spt = ctx.enter_context(nc.sbuf_tensor("spt", [128, LC], F32))
        delta_r = ctx.enter_context(nc.sbuf_tensor("delta_r", [128, 2, LC], F32))
        dA_r = ctx.enter_context(nc.sbuf_tensor("dA_r", [128, 3, LC], F32))
        dBu_sb = ctx.enter_context(nc.sbuf_tensor("dBu_sb", [128, LC], BF16))
        h32_r = ctx.enter_context(nc.sbuf_tensor("h32_r", [128, 2, LC], F32))
        hbf_r = ctx.enter_context(nc.sbuf_tensor("hbf_r", [128, 4, LC], BF16))
        tmp_r = ctx.enter_context(nc.sbuf_tensor("tmp_r", [128, 2, LC], BF16))
        cvo_r = ctx.enter_context(nc.sbuf_tensor("cvo_r", [128, 2, LC], F32))
        xzx_r = ctx.enter_context(nc.sbuf_tensor("xzx_r", [128, 2, LC + 4], BF16))
        dtl_sb = ctx.enter_context(nc.sbuf_tensor("dtl_sb", [128, LC], BF16))
        bc_sb = ctx.enter_context(nc.sbuf_tensor("bc_sb", [32, LC], F32))
        bcb_sb = ctx.enter_context(nc.sbuf_tensor("bcb_sb", [32, LC], BF16))
        u_r = ctx.enter_context(nc.sbuf_tensor("u_r", [128, 2, LC], BF16))
        t1y_r = ctx.enter_context(nc.sbuf_tensor("t1y_r", [128, 2, LC], BF16))
        cva = ctx.enter_context(nc.sbuf_tensor("cva", [128, LC], F32))
        wbufW = ctx.enter_context(nc.sbuf_tensor("wbufW", [128, 80, 128], BF16))
        mo_r = Cb_sb[:, 0:2, :]
        sz_r = Cb_sb[:, 2:4, :]
        t1yi = Cb_sb[:, 4:6, :]
        dwT_sb_full = Cb_sb[:, 6:7, :]
        dwb_sb = Cb_sb[:, 8:16, :]
        gt_sb = ctx.enter_context(nc.sbuf_tensor("gt_sb", [128, 128], F32))
        if True:

            if dbg:
                b.wait("sync", s_v, s_v.n)
                b.dma("sync", dbg_A, A_sb[:], s_ld)
            b.dma("gpsimd", xpT_sb[:], xpT, s_w0)
            b.wait("gpsimd", s_w0, s_w0.n)
            b.dma("gpsimd", dtT_sb[:], dtTw, s_w0)
            b.wait("gpsimd", s_w0, s_w0.n)
            b.dma("gpsimd", gwT_sb[:], gwTi, s_w0)
            b.wait("vector", s_w0, s_w0.n)
            b.wait("vector", s_g, consts_done)
            vready = b.op("vector", lambda e: e.tensor_copy(lnb[:, 9:10], lnb[:, 8:9]), s_v)
            for eng in ("scalar", "tensor", "sync", "gpsimd"):
                b.wait(eng, s_v, vready)

            for tile in range(nt):
                own = tile >= npre
                ot = tile - npre
                if dbg:
                    b.wait("sync", s_v, s_v.n)
                    b.dma("sync", dbg_cr[tile], carry[:], s_ld)
                    b.wait("vector", s_ld, s_ld.n)
                b.wait("scalar", s_xn1s, s_xn1s.n)
                k = L.get("scanv")
                if k is not None:
                    b.wait("scalar", s_v, k)
                for tb in range(4):
                    xs = tile * LC + tb * 128
                    sl = tb % 2
                    k = L.get(("xrg", sl))
                    if k is not None:
                        b.wait("sync", s_v, k)
                    b.dma("sync", x_rg[:, sl, :], xin[xs:xs + 128, :], s_x[sl])
                    ldv = s_x[sl].n
                    b.wait("scalar", s_x[sl], ldv)
                    k = L.get("sq_v")
                    if k is not None:
                        b.wait("scalar", s_v, k)
                    k = L.get(("xn", sl))
                    if k is not None:
                        b.wait("scalar", s_t, k)
                    sqv = b.op("scalar", lambda e, sl=sl: e.activation(
                        xn_bf[:, sl, :], x_rg[:, sl, :], AF.Square, bias=0.0,
                        scale=1.0, accum_out=lnb[:, 1:2]), s_s)
                    b.wait("vector", s_x[sl], ldv)
                    b.wait("vector", s_s, sqv)

                    b.op("vector", lambda e, sl=sl: e.reduce_sum(
                        lnb[:, 0:1], x_rg[:, sl, :], axis=AX), s_v)
                    b.op("vector", lambda e: e.tensor_scalar_mul(
                        lnb[:, 2:3], lnb[:, 0:1], 1.0 / DM), s_v)
                    b.op("vector", lambda e: e.tensor_scalar_mul(
                        lnb[:, 3:4], lnb[:, 1:2], 1.0 / DM), s_v)
                    b.op("vector", lambda e: e.tensor_tensor(
                        lnb[:, 4:5], lnb[:, 2:3], lnb[:, 2:3], OP.mult), s_v)
                    lv = b.op("vector", lambda e: e.tensor_tensor(
                        lnb[:, 5:6], lnb[:, 3:4], lnb[:, 4:5], OP.subtract), s_v)
                    L["sq_v"] = lv

                    def rstd(e):
                        e.activation(lnb[:, 6:7], lnb[:, 5:6], AF.Ln, bias=epsc[:, 0:1], scale=1.0)
                        return e.activation(lnb[:, 7:8], lnb[:, 6:7], AF.Exp, bias=0.0,
                                            scale=-0.5)
                    b.wait("scalar", s_v, lv)
                    rsv = b.op("scalar", rstd, s_s)
                    b.wait("vector", s_s, rsv)
                    k = L.get(("xn", sl))
                    if k is not None:
                        b.wait("vector", s_t, k)
                    xnv = b.op("vector", lambda e, sl=sl: e.tensor_scalar(
                        xn_bf[:, sl, :], x_rg[:, sl, :], lnb[:, 2:3], lnb[:, 7:8],
                        OP.subtract, OP.mult), s_v)
                    L[("xrg", sl)] = xnv
                    xnv_pt = b.pt()
                    for g in range(4):
                        pst = PST1 if g % 2 == 0 else PST2

                        def tr(e, g=g, sl=sl, pst=pst):
                            ins = None
                            for j in range(4):
                                dc = g * 4 + j
                                ins = e.matmul(pst[:, j * 128:(j + 1) * 128],
                                               xn_bf[:, sl, dc * 128:(dc + 1) * 128],
                                               ident[:], start=True, stop=True)
                            return ins
                        tv = b.op("tensor", tr, s_t, upto=pmax(
                            xnv_pt, P.get(("LNpst", g % 2))))

                        def ev(e, g=g, tb=tb, pst=pst):
                            return e.copy(
                                xnT[:, g * 4:(g + 1) * 4, tb * 128:tb * 128 + 128],
                                pst[:, 0:512].rearrange("p (a c) -> p a c", a=4))
                        evv = b.op("scalar", ev, s_s)
                        P[("LNpst", g % 2)] = b.pt()
                        L[("pst", g % 2)] = evv
                        L["xnT_wr"] = evv
                    L[("xn", sl)] = s_t.n
                s_xnT = s_s.n

                if own:
                    b.wait("sync", s_s, s_xnT)
                    for g in range(16):
                        L["sp1"] = b.dma("sync", sp_xn1[ot, g],
                                         xnT[:, g, :], s_xn1s)

                ln_pt = b.pt()
                P = b.last
                for mblk in range(NDB):
                    ws = mblk % 2
                    b.dma("gpsimd", wbufW[:, ws * 16:(ws + 1) * 16, :],
                          wxT[mblk], s_w[ws], upto=P.get(("Awrd", ws)))
                    P[("Awld", ws)] = b.pt()
                    ps = PSA if mblk % 2 == 0 else PSB

                    def mmx(e, ws=ws, ps=ps):
                        ins = None
                        for kc in range(16):
                            ins = e.matmul(ps[:, 0:LC], wbufW[:, ws * 16 + kc, :], xnT[:, kc, :],
                                           start=(kc == 0), stop=(kc == 15))
                        return ins
                    tv = b.op("tensor", mmx, s_t, upto=pmax(
                        P[("Awld", ws)], ln_pt, P.get(("Aev", mblk % 2))))
                    P[("Awrd", ws)] = b.pt()
                    xsl = mblk % 2

                    def evx(e, sl=xsl, ps=ps, db=mblk):
                        e.copy(xzx_r[:, sl, 4:4 + LC], ps[:, 0:LC])
                        return e.copy(xzx_r[:, sl, 1:4], halo[:, db, 1:4])
                    evv = b.op("scalar", evx, s_s, upto=pmax(
                        P[("Awrd", ws)], P.get(("Acv", xsl))))
                    P[("Aev", mblk % 2)] = b.pt()

                    def svh(e, sl=xsl, db=mblk):
                        return e.copy(halo[:, db, 1:4], xzx_r[:, sl, 1 + LC:4 + LC])
                    sv = b.op("scalar", svh, s_s, upto=P[("Aev", mblk % 2)])
                    svh_pt = b.pt()
                    csl = mblk % 2

                    b.op("vector", lambda e, sl=xsl, db=mblk: e.tensor_scalar(
                        cva[:], xzx_r[:, sl, 1:1 + LC],
                        cw_sb[:, db * 4:db * 4 + 1], None, OP.mult), s_v,
                        upto=pmax(svh_pt, P.get(("Asl", csl))))
                    b.op("vector", lambda e, sl=xsl, db=mblk, cs=csl:
                         e.scalar_tensor_tensor(
                             cvo_r[:, cs, :], xzx_r[:, sl, 2:2 + LC],
                             cw_sb[:, db * 4 + 1:db * 4 + 2],
                             cva[:], OP.mult, OP.add), s_v)
                    b.op("vector", lambda e, sl=xsl, db=mblk, cs=csl:
                         e.scalar_tensor_tensor(
                             cva[:], xzx_r[:, sl, 3:3 + LC],
                             cw_sb[:, db * 4 + 2:db * 4 + 3],
                             cvo_r[:, cs, :], OP.mult, OP.add), s_v)
                    cvv = b.op("vector", lambda e, sl=xsl, db=mblk, cs=csl:
                               e.scalar_tensor_tensor(
                                   cvo_r[:, cs, :], xzx_r[:, sl, 4:4 + LC],
                                   cw_sb[:, db * 4 + 3:db * 4 + 4], cva[:],
                                   OP.mult, OP.add), s_v)
                    P[("Acv", xsl)] = b.pt()
                    scv = b.op("scalar", lambda e, db=mblk, cs=csl: e.activation(
                        xc_sb[:, db, :], cvo_r[:, cs, :], AF.Silu, bias=0.0, scale=1.0),
                        s_s, upto=P[("Acv", xsl)])
                    P[("Asl", csl)] = b.pt()
                s_xc = s_s.n
                if dbg and tile == nt - 1:
                    b.wait("sync", s_s, s_xc)
                    for db in range(NDB):
                        b.dma("sync", dbg_xc[db], xc_sb[:, db, :], s_ld)

                b.wait("tensor", s_s, s_xc)

                def xd1(e):
                    ins = None
                    for kc in range(NDB):
                        ins = e.matmul(PSA[:, 0:LC], xpT_sb[:, kc, 0:128],
                                       xc_sb[:, kc, :], start=(kc == 0),
                                       stop=(kc == NDB - 1))
                    return ins
                tv1 = b.op("tensor", xd1, s_t)
                b.wait("scalar", s_t, tv1)
                dtlv = b.op("scalar", lambda e: e.copy(dtl_sb[:], PSA[:, 0:LC]), s_s)

                def xd2(e):
                    ins = None
                    for kc in range(NDB):
                        ins = e.matmul(PSB[0:32, 0:LC], xpT_sb[:, kc, 128:160],
                                       xc_sb[:, kc, :], start=(kc == 0),
                                       stop=(kc == NDB - 1))
                    return ins
                tv2 = b.op("tensor", xd2, s_t)
                b.wait("scalar", s_t, tv2)
                b.op("scalar", lambda e: e.copy(bc_sb[:], PSB[0:32, 0:LC]), s_s)
                bcv2 = b.op("vector", lambda e: e.tensor_copy(bcb_sb[:], bc_sb[:]), s_v)
                if dbg and tile == nt - 1:
                    b.wait("sync", s_v, bcv2)
                    b.dma("sync", dbg_bc, bc_sb[:], s_ld)
                bcb_pt = b.pt()
                sscall = pmax(P.get(("Ssc", 0)), P.get(("Ssc", 1)),
                              P.get(("Ssc", 2)))
                nrows = 32 if own else 16
                for n in range(nrows):
                    psb_ = PSD if n % 2 == 0 else PSE

                    def bbm(e, n=n, psb_=psb_):
                        return e.matmul(psb_[:, 0:LC], sel32[:, n, :], bcb_sb[:],
                                        start=True, stop=True)
                    tv = b.op("tensor", bbm, s_t, upto=pmax(
                        bcb_pt, P.get(("Bev", n % 2)), P.get("St1c")))
                    bbm_pt = b.pt()
                    dst = Bb_sb[:, n, :] if n < 16 else Cb_sb[:, n - 16, :]
                    bev = b.op("scalar", lambda e, d=dst, psb_=psb_: e.copy(
                        d, psb_[:, 0:LC]), s_s, upto=pmax(bbm_pt, sscall))
                    P[("Bev", n % 2)] = b.pt()
                s_bc = s_s.n
                b.wait("vector", s_s, s_bc)
                bc_pt = b.pt()

                for db in range(NDB):
                    dsl = db % 2
                    ps = PSA if db % 2 == 0 else PSB
                    k = L.get(("dtev", dsl))
                    if k is not None:
                        b.wait("tensor", s_s, k)
                    if db == 0:
                        b.wait("tensor", s_s, dtlv)
                    if db == 1:
                        b.wait("tensor", s_v, bcv2)

                    def dtm(e, db=db, ps=ps):
                        return e.matmul(ps[:, 0:LC], dtT_sb[:, db, :], dtl_sb[:],
                                        start=True, stop=True)
                    tv = b.op("tensor", dtm, s_t, upto=pmax(
                        bc_pt, P.get(("Sdt", dsl))))
                    dtm_pt = b.pt()

                    def softp(e, db=db, sl=dsl, ps=ps):
                        e.activation(spt[:], ps[:, 0:LC], AF.Exp,
                                     bias=dtb_sb[:, db:db + 1], scale=1.0)
                        return e.activation(delta_r[:, sl, :], spt[:], AF.Ln,
                                            bias=onec[:, 0:1], scale=1.0)
                    spv = b.op("scalar", softp, s_s, upto=pmax(
                        dtm_pt, P.get(("Suv", dsl))))
                    P[("Sdt", dsl)] = b.pt()
                    spv_pt = b.pt()
                    L[("dtev", dsl)] = spv
                    if dbg and tile == nt - 1 and db < 2:
                        b.wait("sync", s_s, spv)
                        b.dma("sync", dbg_dl[db], delta_r[:, dsl, :], s_ld)
                    usl = db % 2
                    uv = b.op("vector", lambda e, sl=dsl, us=usl, db=db: e.tensor_tensor(
                        u_r[:, us, :], delta_r[:, sl, :], xc_sb[:, db, :], OP.mult), s_v,
                        upto=pmax(spv_pt, bc_pt))
                    P[("Suv", dsl)] = b.pt()
                    L[("delta", dsl)] = s_v.n

                    lastdelta_use = None
                    for n in range(DN):
                        das = n % 3
                        col = db * DN + n
                        dav = b.op("scalar", lambda e, sl=dsl, da=das, col=col:
                                   e.activation(dA_r[:, da, :], delta_r[:, sl, :],
                                                AF.Exp, bias=0.0,
                                                scale=A_sb[:, col:col + 1]), s_s,
                                   upto=pmax(P.get(("Ssc", das)), bc_pt))
                        dav_pt = b.pt()
                        if dbg and tile == nt - 1 and db == 0:
                            b.wait("sync", s_s, dav)
                            dmpv = b.dma("sync", dbg_da[n], dA_r[:, das, :], s_ld)
                            b.wait("scalar", s_ld, dmpv)
                        if own:
                            hs = n % 4
                            ts = n % 2

                            def sct(e, us=usl, n=n, da=das, hs=hs, col=col,
                                    ts=ts):
                                e.tensor_tensor(dBu_sb[:], u_r[:, us, :],
                                                Bb_sb[:, n, :], OP.mult)
                                e.tensor_tensor_scan(
                                    hbf_r[:, hs, :], dA_r[:, da, :], dBu_sb[:],
                                    carry[:, col:col + 1], OP.mult, OP.add)
                                return e.tensor_tensor(
                                    tmp_r[:, ts, :], hbf_r[:, hs, :],
                                    Cb_sb[:, n, :], OP.mult)
                            scv = b.op("vector", sct, s_v, upto=pmax(
                                dav_pt, bc_pt, P.get(("Sy", ts))))
                            P[("Ssc", das)] = b.pt()
                            sct_pt = b.pt()
                            L[("dA", das)] = scv
                            if dbg and tile == nt - 1 and db == 0:
                                b.wait("sync", s_v, scv)
                                b.dma("sync", dbg_h[n], hbf_r[:, hs, :], s_ld)
                            if n % 4 == 3:
                                b.op("vector", lambda e, db=db, n0=n - 3: e.tensor_copy(
                                    carry[:, db * DN + n0:db * DN + n0 + 4],
                                    hbf_r[:, :, LC - 1:LC]),
                                    s_v, upto=sct_pt)
                            yv = b.op("tensor", lambda e, n=n, ts=ts: e.matmul(
                                PSE[:, 0:LC], ident[:], tmp_r[:, ts, :],
                                start=(n == 0), stop=(n == DN - 1)), s_t,
                                upto=pmax(sct_pt,
                                          pmax(P.get("St1c"),
                                               P.get(("Bev", 1)))
                                          if n == 0 else None))
                            P[("Sy", ts)] = b.pt()
                            L[("tmp", ts)] = yv
                        else:
                            hs = n % 4

                            def sct(e, us=usl, n=n, da=das, hs=hs, col=col):
                                e.tensor_tensor(dBu_sb[:], u_r[:, us, :],
                                                Bb_sb[:, n, :], OP.mult)
                                return e.tensor_tensor_scan(
                                    hbf_r[:, hs, :], dA_r[:, da, :], dBu_sb[:],
                                    carry[:, col:col + 1], OP.mult, OP.add)
                            scv = b.op("vector", sct, s_v, upto=pmax(
                                dav_pt, bc_pt))
                            P[("Ssc", das)] = b.pt()
                            sct_pt = b.pt()
                            L[("dA", das)] = scv
                            if dbg and tile == 0 and db == 0 and n == 1:
                                b.wait("sync", s_v, scv)
                                b.dma("sync", dbg_sc[3], hbf_r[:, hs, :], s_ld)
                                b.wait("vector", s_ld, s_ld.n)
                            if n % 4 == 3:
                                b.op("vector", lambda e, db=db, n0=n - 3: e.tensor_copy(
                                    carry[:, db * DN + n0:db * DN + n0 + 4],
                                    hbf_r[:, :, LC - 1:LC]),
                                    s_v, upto=sct_pt)
                    L[("u", usl)] = s_v.n
                    if own:
                        t1s = db % 2
                        b.op("scalar", lambda e, ts=t1s: e.copy(
                            t1y_r[:, ts, :], PSE[:, 0:LC]), s_s,
                            upto=pmax(P.get(("Sy", 1)),
                                      P.get(("Std", t1s))))
                        P["St1c"] = b.pt()
                        t1c_pt = b.pt()
                        t1v = b.op("vector", lambda e, db=db, ts=t1s:
                                   e.scalar_tensor_tensor(
                                       t1y_r[:, ts, :], xc_sb[:, db, :],
                                       D_sb[:, db:db + 1], t1y_r[:, ts, :],
                                       OP.mult, OP.add), s_v, upto=t1c_pt)
                        L["pse"] = t1v
                        L[("t1y", t1s)] = b.dma("sync", sp_t1y[ot, db],
                                                t1y_r[:, t1s, :], s_t1s[t1s])
                        P[("Std", t1s)] = b.pt()
                L["scanv"] = s_v.n

        b.wait("vector", s_st, s_st.n)
        b.wait("vector", s_s, s_s.n)
        b.wait("vector", s_t, s_t.n)
        barv = b.op("vector", lambda e: e.tensor_copy(lnb[:, 10:11], lnb[:, 9:10]), s_v)
        for eng in ("scalar", "tensor", "sync", "gpsimd"):
            b.wait(eng, s_v, barv)
        mo_r = Cb_sb[:, 0:2, :]
        sz_r = Cb_sb[:, 2:4, :]
        t1yi = Cb_sb[:, 4:6, :]
        dwT_sb = Cb_sb[0:8, 6, :]
        dwb4 = Cb_sb[:, 7, 0:8]
        dwb_sb = Cb_sb[:, 8:16, :]
        hd_r = hbf_r[:, 0:2, :]
        gtf = gt_sb[:]
        xo_r = [x_rg[:, 0, 0:512].rearrange("p (a c) -> p a c", a=4),
                x_rg[:, 0, 512:1024].rearrange("p (a c) -> p a c", a=4)]
        ob_r = [x_rg[:, 1, 0:512].rearrange("p (a c) -> p a c", a=4),
                x_rg[:, 1, 512:1024].rearrange("p (a c) -> p a c", a=4)]

        def xo_t(s, tb):
            return x_rg[:, 0, s * 512 + tb * 128:s * 512 + (tb + 1) * 128]

        def ob_t(s, tb):
            return x_rg[:, 1, s * 512 + tb * 128:s * 512 + (tb + 1) * 128]

        for ot in range(nown):
            b.wait("sync", s_xn1s, s_xn1s.n)
            b.wait("sync", s_t, L.get("xnT_rd", 0))
            b.wait("sync", s_s, L.get("xnT_wr", 0))
            for g in range(16):
                b.dma("sync", xnT[:, g, :], sp_xn1[ot, g], s_xn1l)
            ld_xn1 = s_xn1l.n
            xl_pt = b.pt()
            for mblk in range(NDB):
                ws = mblk % 2
                b.dma("gpsimd", wbufW[:, ws * 16:(ws + 1) * 16, :], wzT[mblk],
                      s_w[ws], upto=pmax(P.get(("Awrd", ws)),
                                         P.get(("Zwr", ws)),
                                         P.get(("Ewr", ws)),
                                         P.get(("Owr", ws))))
                zld_pt = b.pt()
                ps = PSA if mblk % 2 == 0 else PSB

                def mmz(e, ws=ws, ps=ps):
                    ins = None
                    for kc in range(16):
                        ins = e.matmul(ps[:, 0:LC], wbufW[:, ws * 16 + kc, :],
                                       xnT[:, kc, :],
                                       start=(kc == 0), stop=(kc == 15))
                    return ins
                tv = b.op("tensor", mmz, s_t, upto=pmax(
                    zld_pt, xl_pt, P.get(("Zev", mblk % 2))))
                P[("Zwr", ws)] = b.pt()
                zmm_pt = b.pt()
                szv = b.op("scalar", lambda e, sl=mblk % 2, ps=ps: e.activation(
                    sz_r[:, sl, :], ps[:, 0:LC], AF.Silu, bias=0.0, scale=1.0),
                    s_s, upto=pmax(zmm_pt, P.get(("Zsz", mblk % 2)),
                                   P.get(("Esz", mblk % 2))))
                P[("Zev", mblk % 2)] = b.pt()
                zsz_pt = b.pt()
                tsl = mblk % 2
                b.dma("sync", t1yi[:, tsl, :], sp_t1y[ot, mblk], s_ti[tsl],
                      upto=P.get(("Zt1", tsl)))
                zt1_pt = b.pt()
                ygv = b.op("vector", lambda e, sl=mblk % 2, db=mblk:
                           e.tensor_tensor(xc_sb[:, db, :], t1yi[:, sl, :],
                                           sz_r[:, sl, :], OP.mult), s_v,
                           upto=pmax(zsz_pt, zt1_pt, P.get("Oxc")))
                P[("Zsz", mblk % 2)] = b.pt()
                P[("Zt1", tsl)] = b.pt()
            yg_done = s_v.n
            yg_pt = b.pt()

            for dmb in range(NKD):
                ws = dmb % 2
                wupto = pmax(P.get(("Awrd", ws)), P.get(("Zwr", ws)),
                             P.get(("Ewr", ws)), P.get(("Owr", ws)))
                b.dma("gpsimd", wbufW[:, 32 + ws * 16:32 + (ws + 1) * 16, :],
                      woT[dmb, :, 0:16, :], s_w[ws], upto=wupto)
                b.dma("gpsimd", wbufW[:, ws * 16:(ws + 1) * 16, :],
                      woT[dmb, :, 16:32, :], s_w[ws], upto=wupto)
                old_pt = b.pt()
                xsl = dmb % 2
                b.dma("sync", xo_r[xsl],
                      xin[(npre + ot) * LC:(npre + ot + 1) * LC,
                          dmb * 128:(dmb + 1) * 128]
                      .rearrange("(a p) c -> p a c", p=128), s_xo[xsl],
                      upto=P.get(("Oxo", xsl)))
                oxo_pt = b.pt()
                ps = PSA if dmb % 2 == 0 else PSB

                def mmo(e, ws=ws, ps=ps):
                    ins = None
                    for kc in range(32):
                        w = (wbufW[:, 32 + ws * 16 + kc, :] if kc < 16
                             else wbufW[:, ws * 16 + kc - 16, :])
                        ins = e.matmul(ps[:, 0:LC], w, xc_sb[:, kc, :],
                                       start=(kc == 0), stop=(kc == 31))
                    return ins
                tv = b.op("tensor", mmo, s_t, upto=pmax(
                    old_pt, yg_pt, P.get(("Oev", dmb % 2))))
                P[("Owr", ws)] = b.pt()
                omm_pt = b.pt()
                mev = b.op("scalar", lambda e, sl=dmb % 2, ps=ps: e.copy(
                    mo_r[:, sl, :], ps[:, 0:LC]), s_s, upto=pmax(
                    omm_pt, P.get(("Omo", dmb % 2)),
                    P.get(("Emo", dmb % 2))))
                P[("Oev", dmb % 2)] = b.pt()
                omev_pt = b.pt()
                pst = PST1 if dmb % 2 == 0 else PST2

                def trm(e, sl=dmb % 2, pst=pst):
                    ins = None
                    for j in range(4):
                        ins = e.matmul(pst[:, j * 128:(j + 1) * 128],
                                       mo_r[:, sl, j * 128:(j + 1) * 128],
                                       ident[:], start=True, stop=True)
                    return ins
                trv = b.op("tensor", trm, s_t, upto=pmax(
                    omev_pt, P.get(("Opst", dmb % 2))))
                P[("Omo", dmb % 2)] = b.pt()
                otr_pt = b.pt()
                osl = dmb % 2
                obe = b.op("scalar", lambda e, pst=pst, os=osl: e.copy(
                    x_rg[:, 1, os * 512:(os + 1) * 512], pst[:, 0:512]), s_s,
                    upto=pmax(otr_pt, P.get(("Oob", osl))))
                oob_pt = b.pt()

                def addm(e, xs=xsl, os=osl):
                    ins = None
                    for tb in range(4):
                        ins = e.tensor_tensor(
                            ob_t(os, tb), xo_t(xs, tb),
                            ob_t(os, tb), OP.add)
                    return ins
                av = b.op("vector", addm, s_v, upto=pmax(oob_pt, oxo_pt))
                P[("Opst", dmb % 2)] = b.pt()
                P[("Oxo", xsl)] = b.pt()
                oav_pt = b.pt()
                L[("obB", osl)] = b.dma(
                    "sync",
                    sp_x2[ot, :, :, dmb * 128:(dmb + 1) * 128]
                    .rearrange("a p c -> p a c"),
                    ob_r[osl], s_ob[osl], upto=oav_pt)
                P[("Oob", osl)] = b.pt()
            P["Oxc"] = b.pt()
            b.wait("sync", s_ob[0], s_ob[0].n)
            b.wait("sync", s_ob[1], s_ob[1].n)

            for tb in range(4):
                sl = tb % 2
                k = L.get(("xrg", sl))
                if k is not None:
                    b.wait("sync", s_v, k)
                b.dma("sync", x_rg[:, sl, :], sp_x2[ot, tb], s_x[sl])
                ldv = s_x[sl].n
                b.wait("scalar", s_x[sl], ldv)
                k = L.get("sq_v")
                if k is not None:
                    b.wait("scalar", s_v, k)
                k = L.get(("xn", sl))
                if k is not None:
                    b.wait("scalar", s_t, k)
                sqv = b.op("scalar", lambda e, sl=sl: e.activation(
                    xn_bf[:, sl, :], x_rg[:, sl, :], AF.Square, bias=0.0,
                    scale=1.0, accum_out=lnb[:, 1:2]), s_s)
                b.wait("vector", s_x[sl], ldv)
                b.wait("vector", s_s, sqv)

                b.op("vector", lambda e, sl=sl: e.reduce_sum(
                    lnb[:, 0:1], x_rg[:, sl, :], axis=AX), s_v)
                b.op("vector", lambda e: e.tensor_scalar_mul(
                    lnb[:, 2:3], lnb[:, 0:1], 1.0 / DM), s_v)
                b.op("vector", lambda e: e.tensor_scalar_mul(
                    lnb[:, 3:4], lnb[:, 1:2], 1.0 / DM), s_v)
                b.op("vector", lambda e: e.tensor_tensor(
                    lnb[:, 4:5], lnb[:, 2:3], lnb[:, 2:3], OP.mult), s_v)
                lv = b.op("vector", lambda e: e.tensor_tensor(
                    lnb[:, 5:6], lnb[:, 3:4], lnb[:, 4:5], OP.subtract), s_v)
                L["sq_v"] = lv

                def rstd(e):
                    e.activation(lnb[:, 6:7], lnb[:, 5:6], AF.Ln,
                                 bias=epsc[:, 0:1], scale=1.0)
                    return e.activation(lnb[:, 7:8], lnb[:, 6:7], AF.Exp,
                                        bias=0.0, scale=-0.5)
                b.wait("scalar", s_v, lv)
                rsv = b.op("scalar", rstd, s_s)
                b.wait("vector", s_s, rsv)
                k = L.get(("xn", sl))
                if k is not None:
                    b.wait("vector", s_t, k)
                xnv = b.op("vector", lambda e, sl=sl: e.tensor_scalar(
                    xn_bf[:, sl, :], x_rg[:, sl, :], lnb[:, 2:3], lnb[:, 7:8],
                    OP.subtract, OP.mult), s_v)
                L[("xrg", sl)] = xnv
                xnv2_pt = b.pt()
                for g in range(4):
                    pst = PST1 if g % 2 == 0 else PST2

                    def tr(e, g=g, sl=sl, pst=pst):
                        ins = None
                        for j in range(4):
                            dc = g * 4 + j
                            ins = e.matmul(pst[:, j * 128:(j + 1) * 128],
                                           xn_bf[:, sl, dc * 128:(dc + 1) * 128],
                                           ident[:], start=True, stop=True)
                        return ins
                    tv = b.op("tensor", tr, s_t, upto=pmax(
                        xnv2_pt, P.get(("LNpst", g % 2))))
                    L[("xn", sl)] = tv

                    def ev(e, g=g, tb=tb, pst=pst):
                        return e.copy(
                            xnT[:, g * 4:(g + 1) * 4, tb * 128:tb * 128 + 128],
                            pst[:, 0:512].rearrange("p (a c) -> p a c", a=4))
                    evv = b.op("scalar", ev, s_s)
                    P[("LNpst", g % 2)] = b.pt()
                    L[("pst", g % 2)] = evv
                    L["xnT_wr"] = evv
            xn2_done = s_s.n
            xn2_pt = b.pt()

            b.wait("tensor", s_s, xn2_done)
            k = L.get("psd")
            if k is not None:
                b.wait("tensor", s_s, k)
            k = L.get("pse_g")
            if k is not None:
                b.wait("tensor", s_v, k)

            def lgm(e):
                ins = None
                for kc in range(16):
                    ins = e.matmul(PSD[0:8, 0:LC], gwT_sb[:, kc, :],
                                   xnT[:, kc, :], start=(kc == 0),
                                   stop=(kc == 15))
                return ins
            tvl = b.op("tensor", lgm, s_t)
            b.wait("scalar", s_t, tvl)
            lgv = b.op("scalar", lambda e: e.copy(spt[0:8, 0:LC], PSD[0:8, 0:LC]),
                       s_s)
            L["psd"] = lgv
            for tb in range(4):
                k = L.get("dwb4")
                if k is not None:
                    b.wait("tensor", s_v, k)
                b.wait("tensor", s_s, lgv)
                k = L.get("pse_v")
                if k is not None:
                    b.wait("tensor", s_v, k)
                ttr = b.op("tensor", lambda e, tb=tb: e.matmul(
                    PSE[:, 0:8], spt[0:8, tb * 128:tb * 128 + 128],
                    ident_f32[0:8, 0:8], start=True, stop=True), s_t)
                if dbg and ot == 0:
                    b.op("scalar", lambda e, tb=tb: e.copy(
                        wbufA[:, 1, tb, 0:8], PSE[:, 0:8]), s_s)
                b.wait("vector", s_t, ttr)

                pev = b.op("scalar", lambda e: e.copy(
                    gtf[:, 120:128], PSE[:, 0:8]), s_s)
                b.wait("vector", s_s, pev)

                b.op("vector", lambda e: e.reduce_max(
                    gtf[:, 0:1], gtf[:, 120:128], axis=AX), s_v)
                g1 = b.op("vector", lambda e: e.tensor_scalar_mul(
                    gtf[:, 1:2], gtf[:, 0:1], -1.0), s_v)
                L["pse_v"] = g1
                b.wait("scalar", s_v, g1)
                g2 = b.op("scalar", lambda e: e.activation(
                    gtf[:, 8:16], gtf[:, 120:128], AF.Exp, bias=gtf[:, 1:2],
                    scale=1.0), s_s)
                b.wait("vector", s_s, g2)
                gate3_steps = [
                    lambda e: e.reduce_sum(gtf[:, 2:3], gtf[:, 8:16], axis=AX),
                    lambda e: e.reciprocal(gtf[:, 3:4], gtf[:, 2:3]),
                    lambda e: e.tensor_scalar(gtf[:, 16:24], gtf[:, 8:16],
                                              gtf[:, 3:4], None, OP.mult),
                    lambda e: e.tensor_tensor(gtf[:, 24:32], gtf[:, 16:24],
                                              gb_bc[:], OP.add),
                    lambda e: e.reduce_max(gtf[:, 4:5], gtf[:, 24:32], axis=AX),
                    lambda e: e.tensor_scalar(gtf[:, 32:40], gtf[:, 24:32],
                                              gtf[:, 4:5], None, OP.is_equal),
                    lambda e: e.tensor_tensor_scan(gtf[:, 40:48], ones8[:],
                                                   gtf[:, 32:40], 0.0,
                                                   OP.mult, OP.add),
                    lambda e: e.tensor_scalar(gtf[:, 48:56], gtf[:, 40:48],
                                              1.0, None, OP.is_equal),
                    lambda e: e.tensor_tensor(gtf[:, 56:64], gtf[:, 32:40],
                                              gtf[:, 48:56], OP.mult),
                    lambda e: e.scalar_tensor_tensor(gtf[:, 64:72],
                                                     gtf[:, 56:64],
                                                     m30[:, 0:1], gtf[:, 24:32],
                                                     OP.mult, OP.add),
                    lambda e: e.reduce_max(gtf[:, 5:6], gtf[:, 64:72], axis=AX),
                    lambda e: e.tensor_scalar(gtf[:, 72:80], gtf[:, 64:72],
                                              gtf[:, 5:6], None, OP.is_equal),
                    lambda e: e.tensor_tensor_scan(gtf[:, 80:88], ones8[:],
                                                   gtf[:, 72:80], 0.0,
                                                   OP.mult, OP.add),
                    lambda e: e.tensor_scalar(gtf[:, 88:96], gtf[:, 80:88],
                                              1.0, None, OP.is_equal),
                    lambda e: e.tensor_tensor(gtf[:, 96:104], gtf[:, 72:80],
                                              gtf[:, 88:96], OP.mult),
                    lambda e: e.tensor_tensor(gtf[:, 104:112], gtf[:, 56:64],
                                              gtf[:, 96:104], OP.add),
                    lambda e: e.tensor_tensor(gtf[:, 112:120], gtf[:, 16:24],
                                              gtf[:, 104:112], OP.mult),
                    lambda e: e.tensor_copy(dwb4, gtf[:, 112:120]),
                ]
                for st in gate3_steps:
                    g3 = b.op("vector", st, s_v)
                if dbg and ot == 0:
                    b.op("vector", lambda e, tb=tb: e.tensor_copy(
                        wbufA[:, 0, tb, 0:120], gtf[:, 0:120]), s_v)
                    b.wait("sync", s_v, g3)
                    gdv = b.dma("sync", dbg_gt[tb], gtf[:, 0:120], s_ld)
                    b.wait("vector", s_ld, gdv)
                b.wait("tensor", s_v, g3)
                ttv = b.op("tensor", lambda e: e.matmul(
                    PST1[0:8, 0:128], dwb4, ident[:], start=True, stop=True), s_t)
                L["dwb4"] = ttv
                b.wait("scalar", s_t, ttv)
                dtv = b.op("scalar", lambda e, tb=tb: e.copy(
                    dwT_sb[:, tb * 128:tb * 128 + 128], PST1[0:8, 0:128]), s_s)
                L["dwt"] = dtv
                b.wait("tensor", s_s, dtv)
            if dbg and ot == 0:
                b.dma("sync", dbg_snapg, wbufA[:, 0], s_ld)
                b.dma("sync", dbg_snapP, wbufA[:, 1], s_ld)
            for en in range(8):
                if en == 0:
                    k = L.get("psd")
                    if k is not None:
                        b.wait("tensor", s_s, k)
                    b.wait("tensor", s_s, L["dwt"])

                def dbm(e, en=en):
                    return e.matmul(PSD[:, 0:LC], sel8[:, en, :], dwT_sb[:],
                                    start=True, stop=True)
                tv = b.op("tensor", dbm, s_t)
                b.wait("scalar", s_t, tv)
                dev = b.op("scalar", lambda e, en=en: e.copy(
                    dwb_sb[:, en, :], PSD[:, 0:LC]), s_s)
                b.wait("tensor", s_s, dev)
            dwb_done = s_s.n
            dwb_pt = b.pt()
            if dbg and ot == 0:
                b.wait("sync", s_s, dwb_done)
                b.dma("sync", dbg_dw, dwT_sb[:], s_ld)
                b.dma("sync", dbg_dwb, dwb_sb[:], s_ld)
                b.wait("scalar", s_ld, s_ld.n)

            for en in range(9):
                for ib in range(8):
                    ws = ib % 2
                    src1 = w1T[en, ib] if en < 8 else ws1T[ib]
                    src3 = w3T[en, ib] if en < 8 else ws3T[ib]
                    eupto = pmax(P.get(("Ewr", ws)), P.get(("Owr", ws)),
                                 P.get(("Zwr", ws)), P.get(("Awrd", ws)))
                    b.dma("gpsimd", wbufW[:, ws * 16:(ws + 1) * 16, :], src1,
                          s_w[ws], upto=eupto)
                    b.dma("gpsimd", wbufW[:, 32 + ws * 16:32 + (ws + 1) * 16, :],
                          src3, s_w[ws], upto=eupto)
                    wld_pt = b.pt()

                    def mm13(e, ws=ws):
                        for kc in range(16):
                            e.matmul(PSA[:, 0:LC], wbufW[:, ws * 16 + kc, :],
                                     xnT[:, kc, :],
                                     start=(kc == 0), stop=(kc == 15))
                        ins = None
                        for kc in range(16):
                            ins = e.matmul(PSB[:, 0:LC],
                                           wbufW[:, 32 + ws * 16 + kc, :],
                                           xnT[:, kc, :],
                                           start=(kc == 0), stop=(kc == 15))
                        return ins
                    tv = b.op("tensor", mm13, s_t, upto=pmax(
                        wld_pt, xn2_pt, P.get(("Eev", ws))))
                    P[("Ewr", ws)] = b.pt()
                    mm_pt = b.pt()
                    s1v = b.op("scalar", lambda e, sl=ib % 2: e.activation(
                        sz_r[:, sl, :], PSA[:, 0:LC], AF.Silu, bias=0.0,
                        scale=1.0), s_s, upto=pmax(
                        mm_pt, P.get(("Esz", ib % 2))))
                    w3e = b.op("scalar", lambda e, sl=ib % 2: e.copy(
                        mo_r[:, sl, :], PSB[:, 0:LC]), s_s, upto=pmax(
                        mm_pt, P.get(("Emo", ib % 2))))
                    P[("Eev", ws)] = b.pt()
                    w3e_pt = b.pt()
                    hsl = ib % 2

                    b.op("vector", lambda e, sl=ib % 2: e.tensor_tensor(
                        mo_r[:, sl, :], sz_r[:, sl, :],
                        mo_r[:, sl, :], OP.mult), s_v, upto=pmax(
                        w3e_pt, dwb_pt))
                    P[("Esz", ib % 2)] = b.pt()
                    hm1_pt = b.pt()

                    def hmul(e, sl=ib % 2, en=en, hs=hsl):
                        if en < 8:
                            return e.tensor_tensor(hd_r[:, hs, :],
                                                   mo_r[:, sl, :],
                                                   dwb_sb[:, en, :], OP.mult)
                        return e.tensor_copy(hd_r[:, hs, :], mo_r[:, sl, :])
                    hv = b.op("vector", hmul, s_v, upto=pmax(
                        hm1_pt, dwb_pt, P.get(("Ehd", hsl))))
                    P[("Emo", ib % 2)] = b.pt()
                    hv_pt = b.pt()
                    L[("hd", hsl)] = b.dma("sync", sp_hc[ot, en * 8 + ib],
                                           hd_r[:, hsl, :], s_hd[hsl],
                                           upto=hv_pt)
                    P[("Ehd", hsl)] = b.pt()

        b.wait("vector", s_s, s_s.n)
        b.wait("vector", s_t, s_t.n)
        barv2 = b.op("vector", lambda e: e.tensor_copy(lnb[:, 11:12],
                                                       lnb[:, 10:11]), s_v)
        for eng in ("scalar", "tensor", "sync", "gpsimd"):
            b.wait(eng, s_v, barv2)

        def hslot(j):
            if j < 32:
                return xc_sb[:, j, :]
            if j < 48:
                return Bb_sb[:, j - 32, :]
            if j < 64:
                return xnT[:, j - 48, :]
            if j < 68:
                return hbf_r[:, j - 64, :]
            if j < 70:
                return tmp_r[:, j - 68, :]
            return u_r[:, j - 70, :]

        b.wait("sync", s_hd[0], s_hd[0].n)
        b.wait("sync", s_hd[1], s_hd[1].n)
        for ot in range(nown):
            for j in range(NKC):
                b.dma("sync", hslot(j), sp_hc[ot, j], s_hcl,
                      upto=P.get("Dmm"))
            ld_h = s_hcl.n
            h_pt = b.pt()
            for dmb in range(NKD):
                wupto = pmax(P.get("Dwr"), h_pt)
                b.dma("gpsimd", wbufW[:, 0:36, :],
                      w2T[dmb, :, 0:NKC // 2, :], s_w[dmb % 2], upto=wupto)
                b.dma("gpsimd", wbufW[:, 36:72, :],
                      w2T[dmb, :, NKC // 2:NKC, :], s_w[dmb % 2], upto=wupto)
                dld_pt = b.pt()
                ps = PSA if dmb % 2 == 0 else PSB

                def mmd(e, ps=ps):
                    ins = None
                    for kc in range(NKC):
                        ins = e.matmul(ps[:, 0:LC], wbufW[:, kc, :], hslot(kc),
                                       start=(kc == 0), stop=(kc == NKC - 1))
                    return ins
                tv = b.op("tensor", mmd, s_t, upto=pmax(
                    dld_pt, h_pt, P.get(("Dev", dmb % 2))))
                P["Dwr"] = b.pt()
                P["Dmm"] = b.pt()
                dmm_pt = b.pt()
                mev = b.op("scalar", lambda e, sl=dmb % 2, ps=ps: e.copy(
                    mo_r[:, sl, :], ps[:, 0:LC]), s_s, upto=pmax(
                    dmm_pt, P.get(("Dmo", dmb % 2))))
                P[("Dev", dmb % 2)] = b.pt()
                dmev_pt = b.pt()
                pst = PST1 if dmb % 2 == 0 else PST2

                def trd(e, sl=dmb % 2, pst=pst):
                    ins = None
                    for j in range(4):
                        ins = e.matmul(pst[:, j * 128:(j + 1) * 128],
                                       mo_r[:, sl, j * 128:(j + 1) * 128],
                                       ident[:], start=True, stop=True)
                    return ins
                trv = b.op("tensor", trd, s_t, upto=pmax(
                    dmev_pt, P.get(("Dpst", dmb % 2))))
                P[("Dmo", dmb % 2)] = b.pt()
                dtr_pt = b.pt()
                xsl = dmb % 2
                b.dma("sync", xo_r[xsl],
                      sp_x2[ot, :, :, dmb * 128:(dmb + 1) * 128]
                      .rearrange("a p c -> p a c"), s_xo[xsl],
                      upto=P.get(("Dxo", xsl)))
                dxo_pt = b.pt()
                osl = dmb % 2
                obe = b.op("scalar", lambda e, pst=pst, os=osl: e.copy(
                    x_rg[:, 1, os * 512:(os + 1) * 512], pst[:, 0:512]), s_s,
                    upto=pmax(dtr_pt, P.get(("Dob", osl))))
                dob_pt = b.pt()

                def addo(e, xs=xsl, os=osl):
                    ins = None
                    for tb in range(4):
                        ins = e.tensor_tensor(
                            ob_t(os, tb), xo_t(xs, tb),
                            ob_t(os, tb), OP.add)
                    return ins
                ov = b.op("vector", addo, s_v, upto=pmax(dob_pt, dxo_pt))
                P[("Dpst", dmb % 2)] = b.pt()
                P[("Dxo", xsl)] = b.pt()
                dov_pt = b.pt()
                L[("ob", osl)] = b.dma(
                    "sync",
                    yout[ot * LC:(ot + 1) * LC, dmb * 128:(dmb + 1) * 128]
                    .rearrange("(a p) c -> p a c", p=128),
                    ob_r[osl], s_ob[osl], upto=dov_pt)
                P[("Dob", osl)] = b.pt()


        if SERIAL:
            def _final(e, cnts=tuple(b.ecnt), sems=list(b.esems)):
                for i in range(5):
                    if cnts[i] > 0:
                        e.wait_ge(sems[i].h, cnts[i])
            b.q["gpsimd"].append(_final)
        else:
            b.wait("gpsimd", s_ob[0], s_ob[0].n)
            b.wait("gpsimd", s_ob[1], s_ob[1].n)
            b.wait("gpsimd", s_ld, s_ld.n)
            b.wait("gpsimd", s_w0, s_w0.n)
            b.wait("gpsimd", s_w1, s_w1.n)

        with nc.Block() as block:
            @block.sync
            def _(e):
                for fn in b.q["sync"]:
                    fn(e)

            @block.gpsimd
            def _(e):
                for fn in b.q["gpsimd"]:
                    fn(e)

            @block.vector
            def _(e):
                for fn in b.q["vector"]:
                    fn(e)

            @block.scalar
            def _(e):
                for fn in b.q["scalar"]:
                    fn(e)

            @block.tensor
            def _(e):
                for fn in b.q["tensor"]:
                    fn(e)

    nc._gtot = b.gtot
    return nc


def _tile_lhsT(w, nm, nk):
    M, K = np.asarray(w).shape
    assert M == nm * 128 and K == nk * 128
    return np.ascontiguousarray(
        np.asarray(w).reshape(nm, 128, nk, 128).transpose(0, 3, 2, 1))


def _per_partition(v, cols):
    a = np.asarray(v)
    if a.ndim == 1:
        a = a.reshape(NDB, 128).transpose(1, 0)
        return np.ascontiguousarray(a)
    a = a.reshape(NDB, 128, cols).transpose(1, 0, 2).reshape(128, NDB * cols)
    return np.ascontiguousarray(a)


def stage_inputs(inputs):
    ip = np.asarray(inputs["L0_attn_in_proj_weight"])
    common = dict(
        wxT=_tile_lhsT(ip[:DI], NDB, 16),
        wzT=_tile_lhsT(ip[DI:], NDB, 16),
        xpT=np.ascontiguousarray(
            np.asarray(inputs["L0_attn_x_proj_weight"]).T
            .reshape(NDB, 128, 160).transpose(1, 0, 2)),
        dtTw=np.ascontiguousarray(
            np.asarray(inputs["L0_attn_dt_proj_weight"]).T
            .reshape(128, NDB, 128)),
        woT=_tile_lhsT(np.asarray(inputs["L0_attn_out_proj_weight"]), NKD, 32),
        w1T=np.stack([_tile_lhsT(np.asarray(inputs["L0_ffn_W1"])[e], 8, 16)
                      for e in range(NE)]),
        w3T=np.stack([_tile_lhsT(np.asarray(inputs["L0_ffn_W3"])[e], 8, 16)
                      for e in range(NE)]),
        w2T=_tile_lhsT(
            np.concatenate([np.asarray(inputs["L0_ffn_W2"])[e] for e in range(NE)]
                           + [np.asarray(inputs["L0_ffn_Ws2"])], axis=1),
            NKD, NKC),
        ws1T=_tile_lhsT(np.asarray(inputs["L0_ffn_Ws1"]), 8, 16),
        ws3T=_tile_lhsT(np.asarray(inputs["L0_ffn_Ws3"]), 8, 16),
        gwTi=np.ascontiguousarray(
            np.asarray(inputs["L0_ffn_gate_w"]).T
            .reshape(16, 128, NE).transpose(1, 0, 2)),
        alog=_per_partition(np.asarray(inputs["L0_attn_A_log"]), DN),
        dtbi=_per_partition(np.asarray(inputs["L0_attn_dt_proj_bias"]), 1),
        dvec=_per_partition(np.asarray(inputs["L0_attn_D"]), 1),
        cwin=_per_partition(np.asarray(inputs["L0_attn_conv1d_weight"]), DC),
        gbin=np.ascontiguousarray(np.asarray(inputs["L0_ffn_gate_b"]).reshape(1, NE)),
    )
    bf_keys = {"wxT", "wzT", "xpT", "dtTw", "woT", "w1T", "w3T", "w2T",
               "ws1T", "ws3T", "gwTi"}
    return {k: np.ascontiguousarray(
                v.astype(ml_dtypes.bfloat16 if k in bf_keys else np.float32))
            for k, v in common.items()}


def kernel(**inputs):
    x = np.asarray(inputs["L0_attn_hidden_states"])
    common = stage_inputs(inputs)
    npre, nown = NPRE_DEF, NOWN_DEF
    in_maps = []
    for core in range(8):
        bidx, q = core // 4, core % 4
        own_end = (q + 1) * nown * LC
        pre = q * nown * LC
        xin = np.zeros((npre + nown) * LC * DM, dtype=np.float32).reshape(-1, DM)
        xin[(npre + nown) * LC - own_end:, :] = x[bidx, :own_end, :]
        m = dict(common)
        m["xin"] = np.ascontiguousarray(xin)
        in_maps.append(m)
    nc = build(npre, nown)
    res = run_bass_kernel_spmd(nc, in_maps, core_ids=list(range(8)))
    shards = [res.results[c]["yout"] for c in range(8)]
    out = np.stack(shards).reshape(NB, 4, nown * LC, DM).reshape(NB, LSEQ, DM)
    return out.astype(np.float32)
